# version 17
# baseline (speedup 1.0000x reference)
import numpy as np
from contextlib import ExitStack

import concourse.bass as bass
import concourse.mybir as mybir
from concourse.bass_utils import run_bass_kernel_spmd

F32 = mybir.dt.float32
F16 = mybir.dt.float16
I8 = mybir.dt.int8
ALU = mybir.AluOpType
AF = mybir.ActivationFunctionType

L, DM, DI, NST = 8192, 1024, 2048, 16
NSH = NST // 2
T = 256
NCH = L // T
NT = T // 128
NDT = DI // 128
NKM = DM // 128
GRID = F16
ENGS = ("sync", "tensor", "vector", "scalar", "gpsimd")


class P:

    def __init__(self, nc, st):
        self.nc = nc
        self.st = st
        self.sems = {}
        self.count = {}
        self.prog = {e: [] for e in ENGS}
        self.floor = {e: {} for e in ENGS}
        self.bufs = {}
        for e in ENGS:
            self._sem(e)

    def _sem(self, name):
        if name not in self.sems:
            self.sems[name] = self.st.enter_context(self.nc.semaphore(name))
            self.count[name] = 0
        return name

    def _b(self, h):
        b = self.bufs.get(id(h))
        if b is None:
            b = {"w": {}, "r": []}
            self.bufs[id(h)] = b
        return b

    def _waits(self, eng, ins, outs, war=True):
        waits = {}
        fl = self.floor[eng]

        def need(tag):
            if tag is None:
                return
            s, v = tag
            if fl.get(s, 0) >= v:
                return
            if waits.get(s, 0) < v:
                waits[s] = v

        for h in ins:
            for t in self._b(h)["w"].items():
                need(t)
        for h in outs:
            b = self._b(h)
            if war:
                for t in b["w"].items():
                    need(t)
                for r in b["r"]:
                    need(r)
        wl = sorted(waits.items())
        for s, v in wl:
            fl[s] = v
        return wl

    def _push(self, eng, f, wl, sem_name, inc):
        sems = self.sems

        def replay(e, wl=wl, f=f, sem=sems[sem_name], inc=inc):
            for s, v in wl:
                e.wait_ge(sems[s], v)
            f(e).then_inc(sem, inc)

        self.prog[eng].append(replay)

    def _settle(self, ins, outs, tag, append=False):
        for h in outs:
            b = self._b(h)
            if append:
                b["w"][tag[0]] = tag[1]
            else:
                b["w"] = {tag[0]: tag[1]}
                b["r"] = []
        for h in ins:
            self._b(h)["r"].append(tag)

    def emit(self, eng, f, ins=(), outs=()):
        wl = self._waits(eng, ins, outs)
        self.count[eng] += 1
        self._push(eng, f, wl, eng, 1)
        self._settle(ins, outs, (eng, self.count[eng]))

    def dma(self, eng, fs, ins=(), outs=(), sem=None, append=False):
        if not isinstance(fs, (list, tuple)):
            fs = [fs]
        sem = self._sem(sem)
        wl = self._waits(eng, ins, outs, war=not append)
        first = True
        for f in fs:
            self.count[sem] += 16
            self._push(eng, f, wl if first else [], sem, 16)
            first = False
        self._settle(ins, outs, (sem, self.count[sem]), append=append)

    def wait(self, eng, sem, val):
        sems = self.sems
        self.prog[eng].append(lambda e, s=sem, v=val: e.wait_ge(sems[s], v))


def build_program():
    nc = bass.Bass("TRN2", target_bir_lowering=False)

    hx = nc.dram_tensor("hx", [L, DM], F32, kind="ExternalInput")
    w_in_t = nc.dram_tensor("w_in_t", [32, NKM, 128, 128], F32, kind="ExternalInput")
    w_out_t = nc.dram_tensor("w_out_t", [8, NDT, 128, 128], F32, kind="ExternalInput")
    wq_t = nc.dram_tensor("wq_t", [NKM, NKM, 128, 128], I8, kind="ExternalInput")
    xproj_t = nc.dram_tensor("xproj_t", [2, NDT, 128, 96], F32, kind="ExternalInput")
    dtw_t = nc.dram_tensor("dtw_t", [2, NDT, 64, 128], F32, kind="ExternalInput")
    wconv = nc.dram_tensor("wconv", [2, NDT, 128, 4], F32, kind="ExternalInput")
    cbias = nc.dram_tensor("cbias", [2, NDT, 128], F32, kind="ExternalInput")
    dtbias = nc.dram_tensor("dtbias", [2, NDT, 128], F32, kind="ExternalInput")
    alog = nc.dram_tensor("alog", [2, NDT, 128, NST], F32, kind="ExternalInput")
    dvec = nc.dram_tensor("dvec", [2, NDT, 128], F32, kind="ExternalInput")
    wsc = nc.dram_tensor("wsc", [NKM, 128], F32, kind="ExternalInput")
    fbias = nc.dram_tensor("fbias", [NKM, 128], F32, kind="ExternalInput")
    out_h = nc.dram_tensor("out", [L, DM], F32, kind="ExternalOutput")
    wc_in = nc.dram_tensor("wc_in", [32, NKM, 128, 128], F16)
    wc_out = nc.dram_tensor("wc_out", [8, NDT, 128, 128], F16)
    wc_q = nc.dram_tensor("wc_q", [NKM, NKM, 128, 128], F16)
    x_sp = nc.dram_tensor("x_sp", [DI, L], F16)
    z_sp = nc.dram_tensor("z_sp", [DI, L], F16)
    y_sp = nc.dram_tensor("y_sp", [DI, L], F16)
    part_sp = nc.dram_tensor("part_sp", [L, DM], F32)
    diag_dram = nc.dram_tensor("diag_dram", [2, NDT, 4, 128, 128], F16)

    st = ExitStack()
    sb = lambda name, shape, dt: st.enter_context(nc.sbuf_tensor(name, shape, dt))
    psa = lambda name, shape, dt: st.enter_context(nc.psum_tensor(name, shape, dt))

    ident16 = sb("ident16", [128, 128], F16)
    ident32 = sb("ident32", [128, 128], F32)
    ones32 = sb("ones32", [1, 128], F32)
    xprojT = sb("xprojT", [128, 2 * NDT * 96], F16)
    dtwT = sb("dtwT", [64, 2 * NDT * 128], F16)
    wcv16 = sb("wcv16", [128, 2 * NDT * 4], F16)
    diag_stg = [sb(f"diag{i}", [128, 4 * 128], F16) for i in range(2)]
    diag_bld = diag_stg[0]
    acol = sb("acol", [128, 2 * NDT * NST], F32)
    dcol = sb("dcol", [128, 2 * NDT], F32)
    cbcol = sb("cbcol", [128, 2 * NDT], F32)
    dtbcol = sb("dtbcol", [128, 2 * NDT], F32)
    wscol = sb("wscol", [128, NKM], F32)
    fbcol = sb("fbcol", [128, NKM], F32)
    wstg32 = sb("wstg32", [128, 3072], F32)
    wstg_i8 = sb("wstg_i8", [128, NKM * 128], I8)
    winstg = [sb(f"winstg{i}", [128, NKM * 128], F16) for i in range(2)]
    woutstg = [sb(f"woutstg{i}", [128, NDT * 128], F16) for i in range(2)]

    x_tm = [sb("x_tm0", [128, NT * DM], F32)]
    lncol = sb("lncol", [128, 24], F32)
    xn_tm = sb("xn_tm", [128, NT * DM], F32)
    xn16 = sb("xn16", [128, NT * DM], F16)
    xq16 = sb("xq16", [128, NT * DM], F16)
    xn_fm = sb("xn_fm", [128, NKM * T], F16)
    xq_fm = sb("xq_fm", [128, NKM * T], F16)
    xsrep = sb("xsrep", [128, T], F32)
    ffn_fm = sb("ffn_fm", [128, NKM * T], F16)
    attn_fm = ffn_fm
    x_fm = sb("x_fm", [128, NDT * (T + 3)], F16)
    dall = [sb(f"dall{i}", [128, NDT * T], F16) for i in range(2)]
    zstg = [sb(f"zstg{i}", [128, T], F16) for i in range(2)]
    xc_fm = [sb(f"xc_fm{i}", [128, NDT * T], F16) for i in range(2)]
    dtin = [sb(f"dtin{i}", [64, T], F16) for i in range(2)]
    bcf = [sb(f"bcf{i}", [128, T], F16) for i in range(2)]
    sel = sb("sel", [128, 32 * 128], F16)
    brep = [sb(f"brep{i}", [128, NST * T], F16) for i in range(2)]
    crep = [sb(f"crep{i}", [128, NST * T], F16) for i in range(2)]
    dsc1 = sb("dsc1", [128, T], F32)
    dsc2 = sb("dsc2", [128, T], F32)
    _uc = sb("uclip", [128, T], F16)
    uclip = [_uc, _uc]
    du = [sb(f"du_{i}", [128, T], F16) for i in range(2)]
    ud = [sb(f"ud{i}", [128, T], F32) for i in range(2)]
    dA = [sb(f"dA{i}", [128, NSH * T], GRID) for i in range(3)]
    beta = [sb(f"beta{i}", [128, NSH * T], GRID) for i in range(2)]
    fixb = [sb(f"fixb{i}", [128, NSH], F32) for i in range(2)]
    carry = sb("carry", [128, NDT * NST], F16)
    ystg = [sb(f"ystg{i}", [128, T], F16) for i in range(2)]
    yfl = [sb(f"yfl{i}", [128, T], F16) for i in range(4)]
    zfl = [sb(f"zfl{i}", [128, T], F16) for i in range(4)]
    _ys = sb("ysum", [128, T], F32)
    _gs = sb("gsil", [128, T], F16)
    ysum = [_ys, _ys]
    gsil = [_gs, _gs]
    sgbuf = [sb(f"sgbuf{i}", [128, T], F32) for i in range(2)]
    gated = sb("gated", [128, NDT * T], F16)

    psA = psa("psA", [128, 512], F32)
    psB = psa("psB", [128, 512], F32)
    psC = psa("psC", [128, 512], F32)
    psD = psa("psD", [128, 512], F32)
    psE = psa("psE", [128, 512], F32)
    psF = psa("psF", [128, 512], F32)
    psTA = psa("psTA", [128, 1024], F16)
    psTB = psa("psTB", [128, 1024], F16)

    p = P(nc, st)

    def seg3(t, n=NST):
        return t[:].rearrange("p (n t) -> p n t", n=n)

    def ld(dst_ap, src_ap, outs, slow=False, ins=()):
        p.dma("sync", lambda e, d=dst_ap, s=src_ap, sl=slow: e.dma_start(
            d, s, allow_slow_non_contiguous=sl), ins=ins, outs=outs,
            sem="q_" + outs[0].name)

    p.emit("gpsimd", lambda e: e.memset(ident16[:], 1.0), outs=[ident16])
    p.emit("gpsimd", lambda e: e.affine_select(
        ident16[:], ident16[:], [[-1, 128]], ALU.is_equal, 0.0, base=0,
        channel_multiplier=1), ins=[ident16], outs=[ident16])
    p.emit("gpsimd", lambda e: e.memset(ident32[:], 1.0), outs=[ident32])
    p.emit("gpsimd", lambda e: e.affine_select(
        ident32[:], ident32[:], [[-1, 128]], ALU.is_equal, 0.0, base=0,
        channel_multiplier=1), ins=[ident32], outs=[ident32])
    p.emit("gpsimd", lambda e: e.memset(ones32[:], 1.0), outs=[ones32])

    ld(wstg32[:, 0:2 * NDT * 4].rearrange("p (r dt k) -> p r dt k", r=2, dt=NDT),
       wconv.ap().rearrange("r dt p k -> p r dt k"), [wstg32])
    p.emit("vector", lambda e: e.tensor_copy(wcv16[:], wstg32[:, 0:2 * NDT * 4]),
           ins=[wstg32], outs=[wcv16])
    ld(cbcol[:].rearrange("p (r dt) -> p r dt", r=2),
       cbias.ap().rearrange("r dt p -> p r dt"), slow=True, outs=[cbcol])
    ld(dtbcol[:].rearrange("p (r dt) -> p r dt", r=2),
       dtbias.ap().rearrange("r dt p -> p r dt"), slow=True, outs=[dtbcol])
    ld(dcol[:].rearrange("p (r dt) -> p r dt", r=2),
       dvec.ap().rearrange("r dt p -> p r dt"), slow=True, outs=[dcol])
    ld(wscol[:], wsc.ap().rearrange("m p -> p m"), slow=True, outs=[wscol])
    ld(fbcol[:], fbias.ap().rearrange("m p -> p m"), slow=True, outs=[fbcol])
    ld(acol[:].rearrange("p (r dt n) -> p r dt n", r=2, dt=NDT),
       alog.ap().rearrange("r dt p n -> p r dt n"), [acol])
    p.emit("scalar", lambda e: e.activation(acol[:], acol[:], AF.Exp),
           ins=[acol], outs=[acol])
    p.emit("vector", lambda e: e.tensor_scalar(acol[:], acol[:], -1.0, None, ALU.mult),
           ins=[acol], outs=[acol])
    for r in range(2):
        ld(wstg32[:, 0:NDT * 96].rearrange("p (dt m) -> p dt m", dt=NDT),
           xproj_t.ap()[r].rearrange("dt k m -> k dt m"), [wstg32])
        p.emit("vector", lambda e, r=r: e.tensor_copy(
            xprojT[:, r * NDT * 96:(r + 1) * NDT * 96], wstg32[:, 0:NDT * 96]),
            ins=[wstg32], outs=[xprojT])
        ld(wstg32[0:64, 0:NDT * 128].rearrange("p (dt m) -> p dt m", dt=NDT),
           dtw_t.ap()[r].rearrange("dt k m -> k dt m"), [wstg32])
        p.emit("vector", lambda e, r=r: e.tensor_copy(
            dtwT[:, r * NDT * 128:(r + 1) * NDT * 128],
            wstg32[0:64, 0:NDT * 128]), ins=[wstg32], outs=[dtwT])

    for mt in range(32):
        ld(wstg32[:, 0:NKM * 128].rearrange("p (kt m) -> p kt m", kt=NKM),
           w_in_t.ap()[mt].rearrange("kt k m -> k kt m"), [wstg32])
        p.emit("vector", lambda e: e.tensor_copy(
            winstg[0][:], wstg32[:, 0:NKM * 128]), ins=[wstg32], outs=[winstg[0]])
        p.dma("gpsimd", lambda e, mt=mt: e.dma_start(
            wc_in.ap()[mt].rearrange("kt k m -> k kt m"),
            winstg[0][:].rearrange("p (kt m) -> p kt m", kt=NKM)),
            ins=[winstg[0]], outs=[wc_in], sem="q_wc_in", append=True)
    for mo in range(8):
        for half in range(2):
            ld(wstg32[:, 0:NKM * 128].rearrange("p (a m) -> p a m", a=NKM),
               w_out_t.ap()[mo, half * NKM:(half + 1) * NKM].rearrange(
                   "kt k m -> k kt m"), [wstg32])
            p.emit("vector", lambda e: e.tensor_copy(
                winstg[0][:], wstg32[:, 0:NKM * 128]), ins=[wstg32], outs=[winstg[0]])
            p.dma("gpsimd", lambda e, mo=mo, half=half: e.dma_start(
                wc_out.ap()[mo, half * NKM:(half + 1) * NKM].rearrange(
                    "kt k m -> k kt m"),
                winstg[0][:].rearrange("p (kt m) -> p kt m", kt=NKM)),
                ins=[winstg[0]], outs=[wc_out], sem="q_wc_out", append=True)
    for ki in range(NKM):
        ld(wstg_i8[:].rearrange("p (mo m) -> p mo m", mo=NKM),
           wq_t.ap()[ki].rearrange("mo k m -> k mo m"), [wstg_i8])
        p.emit("vector", lambda e: e.tensor_copy(
            winstg[0][:], wstg_i8[:]),
            ins=[wstg_i8], outs=[winstg[0]])
        p.dma("gpsimd", lambda e, ki=ki: e.dma_start(
            wc_q.ap()[ki].rearrange("mo k m -> k mo m"),
            winstg[0][:].rearrange("p (mo m) -> p mo m", mo=NKM)),
            ins=[winstg[0]], outs=[wc_q], sem="q_wc_q", append=True)

    p.emit("gpsimd", lambda e: e.memset(sel[:], 1.0), outs=[sel])
    for n in range(32):
        p.emit("gpsimd", lambda e, n=n: e.affine_select(
            sel[64:96, n * 128:(n + 1) * 128], sel[64:96, n * 128:(n + 1) * 128],
            [[0, 128]], ALU.is_equal, 0.0, base=-n, channel_multiplier=1),
            ins=[sel], outs=[sel])

    def build_diag(r):
        for dt in range(NDT):
            for k in range(4):
                src_k = k if r == 0 else 3 - k
                wslice = wcv16[:, (r * NDT + dt) * 4 + src_k:
                               (r * NDT + dt) * 4 + src_k + 1]
                dst = diag_bld[:, k * 128:(k + 1) * 128]
                p.emit("gpsimd", lambda e, d=dst, w=wslice: e.affine_select(
                    d, w.broadcast_to((128, 128)), [[-1, 128]], ALU.is_equal, 0.0,
                    base=0, channel_multiplier=1), ins=[wcv16], outs=[diag_bld])
            p.dma("gpsimd", lambda e, r=r, dt=dt: e.dma_start(
                diag_dram.ap()[r, dt].rearrange("k p m -> p k m"),
                diag_bld[:].rearrange("p (k m) -> p k m", k=4)),
                ins=[diag_bld], outs=[diag_dram], sem="q_diag", append=True)

    build_diag(0)
    build_diag(1)

    build_diag(1)

    def stage_load_x(c):
        xb = x_tm[0]
        ld(xb[:].rearrange("p (i d) -> p i d", i=NT),
           hx.ap()[c * T:(c + 1) * T, :].rearrange("(i p) d -> p i d", p=128), [xb])
        return xb

    def stage_ln(c, xb):
        scr = wstg32[:, 2048:3072]
        for tt in range(NT):
            xs = xb[:, tt * DM:(tt + 1) * DM]
            p.emit("scalar", lambda e, xs=xs, scr=scr: e.activation(
                scr, xs, AF.Copy, accum_out=lncol[:, 0:1]),
                ins=[xb], outs=[wstg32, lncol])
            p.emit("scalar", lambda e, xs=xs, scr=scr: e.activation(
                scr, xs, AF.Square, accum_out=lncol[:, 1:2]),
                ins=[xb], outs=[wstg32, lncol])
            p.emit("vector", lambda e: e.tensor_scalar(
                lncol[:, 2:3], lncol[:, 0:1], 1.0 / DM, None, ALU.mult),
                ins=[lncol], outs=[lncol])
            p.emit("vector", lambda e: e.tensor_scalar(
                lncol[:, 3:4], lncol[:, 1:2], 1.0 / DM, None, ALU.mult),
                ins=[lncol], outs=[lncol])
            p.emit("vector", lambda e: e.tensor_tensor(
                lncol[:, 4:5], lncol[:, 2:3], lncol[:, 2:3], ALU.mult),
                ins=[lncol], outs=[lncol])
            p.emit("vector", lambda e: e.tensor_tensor(
                lncol[:, 5:6], lncol[:, 3:4], lncol[:, 4:5], ALU.subtract),
                ins=[lncol], outs=[lncol])
            p.emit("vector", lambda e: e.tensor_scalar(
                lncol[:, 5:6], lncol[:, 5:6], 1e-5, None, ALU.add),
                ins=[lncol], outs=[lncol])
            p.emit("scalar", lambda e: e.activation(
                lncol[:, 6:7], lncol[:, 5:6], AF.Ln), ins=[lncol], outs=[lncol])
            p.emit("scalar", lambda e: e.activation(
                lncol[:, 7:8], lncol[:, 6:7], AF.Exp, scale=-0.5),
                ins=[lncol], outs=[lncol])
            p.emit("vector", lambda e: e.tensor_tensor(
                lncol[:, 10:11], lncol[:, 2:3], lncol[:, 7:8], ALU.mult),
                ins=[lncol], outs=[lncol])
            p.emit("vector", lambda e: e.tensor_scalar(
                lncol[:, 10:11], lncol[:, 10:11], -1.0, None, ALU.mult),
                ins=[lncol], outs=[lncol])
            p.emit("vector", lambda e, xs=xs, tt=tt: e.tensor_scalar(
                xn_tm[:, tt * DM:(tt + 1) * DM], xs, lncol[:, 7:8], lncol[:, 10:11],
                ALU.mult, ALU.add), ins=[xb, lncol], outs=[xn_tm])
        p.emit("vector", lambda e: e.tensor_copy(xn16[:], xn_tm[:]),
               ins=[xn_tm], outs=[xn16])

    def stage_quant(c):
        for tt in range(NT):
            xs = xn_tm[:, tt * DM:(tt + 1) * DM]
            q1s = wstg32[:, 0:DM]
            q2s = wstg32[:, DM:2 * DM]
            p.emit("vector", lambda e, xs=xs: e.tensor_reduce(
                lncol[:, 12:13], xs, mybir.AxisListType.X, ALU.max,
                apply_absolute_value=True), ins=[xn_tm], outs=[lncol])
            p.emit("vector", lambda e: e.tensor_scalar(
                lncol[:, 13:14], lncol[:, 12:13], 1e-6, 1.0 / 127.0, ALU.max, ALU.mult),
                ins=[lncol], outs=[lncol])
            p.emit("vector", lambda e: e.reciprocal(lncol[:, 14:15], lncol[:, 13:14]),
                   ins=[lncol], outs=[lncol])
            p.emit("vector", lambda e, tt=tt: e.tensor_copy(
                lncol[:, 16 + tt:17 + tt], lncol[:, 13:14]), ins=[lncol], outs=[lncol])
            p.emit("vector", lambda e, xs=xs, q1s=q1s: e.tensor_scalar(
                q1s, xs, lncol[:, 14:15], None, ALU.mult),
                ins=[xn_tm, lncol], outs=[wstg32])
            p.emit("vector", lambda e, q1s=q1s: e.tensor_scalar(
                q1s, q1s, -128.0, 127.0, ALU.max, ALU.min), ins=[wstg32], outs=[wstg32])
            p.emit("vector", lambda e, q1s=q1s: e.tensor_scalar(
                q1s, q1s, 128.5, None, ALU.add), ins=[wstg32], outs=[wstg32])
            p.emit("vector", lambda e, q1s=q1s, q2s=q2s: e.tensor_copy(
                q2s, q1s), ins=[wstg32], outs=[wstg32])
            q3s = wstg32[:, 2 * DM:3 * DM]
            for bit in (128.0, 64.0, 32.0, 16.0, 8.0, 4.0, 2.0, 1.0):
                p.emit("vector", lambda e, q2s=q2s, q3s=q3s, b=bit: e.tensor_scalar(
                    q3s, q2s, b, -b, ALU.is_ge, ALU.mult), ins=[wstg32], outs=[wstg32])
                p.emit("vector", lambda e, q2s=q2s, q3s=q3s: e.tensor_tensor(
                    q2s, q2s, q3s, ALU.add), ins=[wstg32], outs=[wstg32])
            p.emit("vector", lambda e, q1s=q1s, q2s=q2s: e.scalar_tensor_tensor(
                q1s, q2s, -1.0, q1s, ALU.mult, ALU.add), ins=[wstg32], outs=[wstg32])
            p.emit("vector", lambda e, q1s=q1s, tt=tt: e.tensor_scalar(
                xq16[:, tt * DM:(tt + 1) * DM], q1s, -128.0, None, ALU.add),
                ins=[wstg32], outs=[xq16])
        p.dma("sync", [
            (lambda e, tt=tt: e.dma_start(
                wstg32[0:1, 2048 + tt * 128:2048 + (tt + 1) * 128], lncol[:, 16 + tt:17 + tt],
                allow_slow_non_contiguous=True))
            for tt in range(NT)], ins=[lncol], outs=[wstg32], sem="q_xs_row")
        p.emit("tensor", lambda e: e.matmul(
            psF[:, 0:T], ones32[:], wstg32[0:1, 2048:2048 + T], start=True, stop=True),
            ins=[ones32, wstg32], outs=[psF])
        p.emit("vector", lambda e: e.tensor_copy(xsrep[:], psF[:, 0:T]),
               ins=[psF], outs=[xsrep])

    def transpose_tm_to_fm(src16, dst16, pst):
        for kt in range(NKM):
            for tt in range(NT):
                slot = ((kt * NT + tt) % 8) * 128
                p.emit("tensor", lambda e, kt=kt, tt=tt, slot=slot, s=src16, pp=pst:
                       e.transpose(pp[:, slot:slot + 128],
                                   s[:, tt * DM + kt * 128:tt * DM + kt * 128 + 128],
                                   ident16[:]), ins=[src16, ident16], outs=[pst])
                p.emit("vector", lambda e, kt=kt, tt=tt, slot=slot, d=dst16, pp=pst:
                       e.tensor_copy(d[:, kt * T + tt * 128:kt * T + tt * 128 + 128],
                                     pp[:, slot:slot + 128]), ins=[pst], outs=[dst16])

    def stage_ffn_mm(c):
        transpose_tm_to_fm(xq16, xq_fm, psTA)
        for mo in range(NKM):
            wb = winstg[mo % 2]
            ld(wb[:].rearrange("p (ki m) -> p ki m", ki=NKM),
               wc_q.ap()[:, mo].rearrange("ki k m -> k ki m"), [wb], ins=[wc_q])
            pp = psA if mo % 2 == 0 else psB
            for ki in range(NKM):
                p.emit("tensor", lambda e, ki=ki, wb=wb, pp=pp: e.matmul(
                    pp[:, 0:T], wb[:, ki * 128:(ki + 1) * 128],
                    xq_fm[:, ki * T:(ki + 1) * T], start=(ki == 0), stop=(ki == NKM - 1)),
                    ins=[wb, xq_fm], outs=[pp])
            p.emit("vector", lambda e, mo=mo, pp=pp: e.tensor_tensor(
                ffn_fm[:, mo * T:(mo + 1) * T], pp[:, 0:T], xsrep[:], ALU.mult),
                ins=[pp, xsrep], outs=[ffn_fm])
            p.emit("scalar", lambda e, mo=mo: e.activation(
                ffn_fm[:, mo * T:(mo + 1) * T], ffn_fm[:, mo * T:(mo + 1) * T],
                AF.Identity, bias=fbcol[:, mo:mo + 1], scale=wscol[:, mo:mo + 1]),
                ins=[ffn_fm, wscol, fbcol], outs=[ffn_fm])

    def stage_ffn_partial(c):
        xb = x_tm[0]
        for o in range(NKM):
            for tt in range(NT):
                pp = psTA if (o * NT + tt) % 2 == 0 else psTB
                p.emit("tensor", lambda e, o=o, tt=tt, pp=pp: e.transpose(
                    pp[:, 0:128], ffn_fm[:, o * T + tt * 128:o * T + tt * 128 + 128],
                    ident16[:]), ins=[ffn_fm, ident16], outs=[pp])
                p.emit("vector", lambda e, o=o, tt=tt, pp=pp, xb=xb: e.tensor_tensor(
                    xb[:, tt * DM + o * 128:tt * DM + o * 128 + 128],
                    pp[:, 0:128], xb[:, tt * DM + o * 128:tt * DM + o * 128 + 128],
                    ALU.add), ins=[pp, xb], outs=[xb])
        p.dma("gpsimd", lambda e, c=c, xb=xb: e.dma_start(
            part_sp.ap()[c * T:(c + 1) * T, :].rearrange("(i p) d -> p i d", p=128),
            xb[:].rearrange("p (i d) -> p i d", i=NT)),
            ins=[xb], outs=[part_sp], sem="q_part_sp", append=True)

    def stage_inproj(c, mts):
        for mt in mts:
            wb = winstg[mt % 2]
            ld(wb[:].rearrange("p (kt m) -> p kt m", kt=NKM),
               wc_in.ap()[mt].rearrange("kt k m -> k kt m"), [wb], ins=[wc_in])
            pp = psA if mt % 2 == 0 else psB
            for kt in range(NKM):
                p.emit("tensor", lambda e, kt=kt, wb=wb, pp=pp: e.matmul(
                    pp[:, 0:T], wb[:, kt * 128:(kt + 1) * 128],
                    xn_fm[:, kt * T:(kt + 1) * T], start=(kt == 0), stop=(kt == NKM - 1)),
                    ins=[wb, xn_fm], outs=[pp])
            if mt < NDT:
                dst = x_fm[:, mt * (T + 3) + 3: mt * (T + 3) + 3 + T]
                p.emit("scalar", lambda e, dst=dst, pp=pp: e.activation(
                    dst, pp[:, 0:T], AF.Copy), ins=[pp], outs=[x_fm])
            else:
                zb = zstg[mt % 2]
                p.emit("scalar", lambda e, zb=zb, pp=pp: e.activation(
                    zb[:], pp[:, 0:T], AF.Copy), ins=[pp], outs=[zb])
                p.dma("gpsimd", lambda e, mt=mt, c=c, zb=zb: e.dma_start(
                    z_sp.ap()[(mt - NDT) * 128:(mt - NDT + 1) * 128, c * T:(c + 1) * T],
                    zb[:]), ins=[zb], outs=[z_sp], sem=f"q_st_z{mt % 2}", append=True)

    def stage_xspill_halo(c, last):
        p.dma("gpsimd", [
            (lambda e, mt=mt, c=c: e.dma_start(
                x_sp.ap()[mt * 128:(mt + 1) * 128, c * T:(c + 1) * T],
                x_fm[:, mt * (T + 3) + 3: mt * (T + 3) + 3 + T]))
            for mt in range(NDT)], ins=[x_fm], outs=[x_sp], sem="q_st_x")
        if not last:
            for dt in range(NDT):
                p.emit("vector", lambda e, dt=dt: e.tensor_copy(
                    x_fm[:, dt * (T + 3):dt * (T + 3) + 3],
                    x_fm[:, dt * (T + 3) + T:dt * (T + 3) + T + 3]),
                    ins=[x_fm], outs=[x_fm])

    def stage_conv(r, g, dts):
        dts = list(dts)
        ld(diag_stg[dts[0] % 2][:].rearrange("p (k m) -> p k m", k=4),
           diag_dram.ap()[r, dts[0]].rearrange("k p m -> p k m"),
           [diag_stg[dts[0] % 2]], ins=[diag_dram])
        for i, dt in enumerate(dts):
            db = diag_stg[dt % 2]
            if i + 1 < len(dts):
                nxt_dt = dts[i + 1]
                ld(diag_stg[nxt_dt % 2][:].rearrange("p (k m) -> p k m", k=4),
                   diag_dram.ap()[r, nxt_dt].rearrange("k p m -> p k m"),
                   [diag_stg[nxt_dt % 2]], ins=[diag_dram])
            pp = psC if dt % 2 == 0 else psD
            for k in range(4):
                p.emit("tensor", lambda e, dt=dt, k=k, pp=pp, db=db: e.matmul(
                    pp[:, 0:T], db[:, k * 128:(k + 1) * 128],
                    x_fm[:, dt * (T + 3) + k: dt * (T + 3) + k + T],
                    start=(k == 0), stop=(k == 3)), ins=[db, x_fm], outs=[pp])
            p.emit("scalar", lambda e, dt=dt, r=r, pp=pp, g=g: e.activation(
                xc_fm[g][:, dt * T:(dt + 1) * T], pp[:, 0:T], AF.Identity,
                bias=cbcol[:, r * NDT + dt:r * NDT + dt + 1]),
                ins=[pp, cbcol], outs=[xc_fm[g]])

    def stage_xdbl(r, g):
        for kt in range(NDT):
            p.emit("tensor", lambda e, kt=kt, r=r, g=g: e.matmul(
                psC[0:96, 0:T], xprojT[:, (r * NDT + kt) * 96:(r * NDT + kt + 1) * 96],
                xc_fm[g][:, kt * T:(kt + 1) * T], start=(kt == 0),
                stop=(kt == NDT - 1)), ins=[xprojT, xc_fm[g]], outs=[psC])
        p.emit("scalar", lambda e, g=g: e.activation(
            dtin[g][:], psC[0:64, 0:T], AF.Copy), ins=[psC], outs=[dtin[g]])
        p.emit("scalar", lambda e, g=g: e.activation(
            bcf[g][64:96, :], psC[64:96, 0:T], AF.Copy), ins=[psC], outs=[bcf[g]])

    def stage_brep(g, which):
        rep = brep[g] if which == 0 else crep[g]
        base = 0 if which == 0 else 16
        for n in range(NST):
            pp = psE if n % 2 == 0 else psF
            p.emit("tensor", lambda e, n=n, pp=pp, g=g: e.matmul(
                pp[:, 0:T], sel[64:96, (base + n) * 128:(base + n + 1) * 128],
                bcf[g][64:96, :], start=True, stop=True),
                ins=[sel, bcf[g]], outs=[pp])
            if n % 2 == 0:
                p.emit("vector", lambda e, n=n, pp=pp, rep=rep: e.tensor_copy(
                    rep[:, n * T:(n + 1) * T], pp[:, 0:T]), ins=[pp], outs=[rep])
            else:
                p.emit("scalar", lambda e, n=n, pp=pp, rep=rep: e.activation(
                    rep[:, n * T:(n + 1) * T], pp[:, 0:T], AF.Copy),
                    ins=[pp], outs=[rep])

    def stage_delta(r, g, dts):
        for dt in dts:
            idx = r * NDT + dt
            pp = psC if dt % 2 == 0 else psD
            p.emit("tensor", lambda e, idx=idx, pp=pp, g=g: e.matmul(
                pp[:, 0:T], dtwT[:, idx * 128:(idx + 1) * 128], dtin[g][:],
                start=True, stop=True), ins=[dtwT, dtin[g]], outs=[pp])
            p.emit("vector", lambda e, idx=idx, pp=pp: e.tensor_scalar(
                dsc1[:], pp[:, 0:T], dtbcol[:, idx:idx + 1], 1e-5, ALU.add, ALU.max),
                ins=[pp, dtbcol], outs=[dsc1])
            p.emit("vector", lambda e, idx=idx: e.tensor_scalar(
                dsc2[:], dsc1[:], 1.0, dtbcol[:, idx:idx + 1], ALU.min, ALU.add),
                ins=[dsc1, dtbcol], outs=[dsc2])
            p.emit("scalar", lambda e: e.activation(dsc1[:], dsc2[:], AF.Exp),
                   ins=[dsc2], outs=[dsc1])
            p.emit("scalar", lambda e, dt=dt, g=g: e.activation(
                dall[g][:, dt * T:(dt + 1) * T], dsc1[:], AF.Ln, bias=1.0),
                ins=[dsc1], outs=[dall[g]])

    def scan_prep(c, r, g, u, rev, gu):
        dt, h = u // 2, u % 2
        idx = r * NDT + dt
        dA_ = dA[gu % 3]
        dslice = dall[g][:, dt * T:(dt + 1) * T]
        if h == 0:
            du_, ud_, uclip_ = du[dt % 2], ud[dt % 2], uclip[dt % 2]
            p.emit("vector", lambda e, dt=dt, uclip_=uclip_, g=g: e.tensor_scalar(
                uclip_[:], xc_fm[g][:, dt * T:(dt + 1) * T], 10.0, -10.0,
                ALU.min, ALU.max), ins=[xc_fm[g]], outs=[uclip_])
            p.emit("vector", lambda e, du_=du_, ds=dslice, uclip_=uclip_, g=g:
                   e.tensor_tensor(du_[:], ds, uclip_[:], ALU.mult),
                   ins=[dall[g], uclip_], outs=[du_])
            p.emit("vector", lambda e, idx=idx, ud_=ud_, uclip_=uclip_:
                   e.tensor_scalar(ud_[:], uclip_[:], dcol[:, idx:idx + 1], None,
                                   ALU.mult), ins=[uclip_, dcol], outs=[ud_])
        din = dslice[:, ::-1] if rev else dslice
        for n in range(NSH):
            na = h * NSH + n
            p.emit("scalar", lambda e, n=n, na=na, idx=idx, din=din, dA_=dA_, g=g:
                   e.activation(dA_[:, n * T:(n + 1) * T], din, AF.Exp,
                                scale=acol[:, idx * NST + na:idx * NST + na + 1]),
                   ins=[dall[g], acol], outs=[dA_])

    def seg3h(t):
        return t[:].rearrange("p (n t) -> p n t", n=NSH)

    def scan_heavy(c, r, g, u, rev, gu):
        dt, h = u // 2, u % 2
        par = u % 2
        dA_, beta_ = dA[gu % 3], beta[par]
        du16_ = du[dt % 2]
        fx = fixb[par]
        bslice = brep[g][:, h * NSH * T:(h + 1) * NSH * T]
        cslice = crep[g][:, h * NSH * T:(h + 1) * NSH * T]
        b3 = bslice.rearrange("p (n t) -> p n t", n=NSH)
        c3 = cslice.rearrange("p (n t) -> p n t", n=NSH)
        if rev:
            du_b = du16_[:, ::-1].unsqueeze(1).broadcast_to((128, NSH, T))
            br_b = b3[:, :, ::-1]
        else:
            du_b = du16_[:].unsqueeze(1).broadcast_to((128, NSH, T))
            br_b = b3
        p.emit("vector", lambda e, du_b=du_b, br_b=br_b, beta_=beta_: e.tensor_tensor(
            seg3h(beta_), du_b, br_b, ALU.mult), ins=[du16_, brep[g]], outs=[beta_])
        cs = carry[:, dt * NST + h * NSH: dt * NST + (h + 1) * NSH]
        p.emit("vector", lambda e, cs=cs, dA_=dA_, fx=fx: e.tensor_tensor(
            fx[:].rearrange("p (n o) -> p n o", o=1),
            seg3h(dA_)[:, :, 0:1], cs.rearrange("p (n o) -> p n o", o=1), ALU.mult),
            ins=[dA_, carry], outs=[fx])
        p.emit("vector", lambda e, beta_=beta_, fx=fx: e.tensor_tensor(
            seg3h(beta_)[:, :, 0:1], seg3h(beta_)[:, :, 0:1],
            fx[:].rearrange("p (n o) -> p n o", o=1), ALU.add),
            ins=[beta_, fx], outs=[beta_])
        p.emit("vector", lambda e, dA_=dA_: e.memset(seg3h(dA_)[:, :, 0:1], 0.0),
               ins=[dA_], outs=[dA_])
        p.emit("vector", lambda e, dA_=dA_, beta_=beta_: e.tensor_tensor_scan(
            dA_[:], dA_[:], beta_[:], 0.0, ALU.mult, ALU.add),
            ins=[dA_, beta_], outs=[dA_])
        p.emit("vector", lambda e, cs=cs, dA_=dA_: e.tensor_copy(
            cs.rearrange("p (n o) -> p n o", o=1), seg3h(dA_)[:, :, T - 1:T]),
            ins=[dA_], outs=[carry])
        cr_b = c3[:, :, ::-1] if rev else c3
        p.emit("vector", lambda e, cr_b=cr_b, dA_=dA_, beta_=beta_: e.tensor_tensor(
            seg3h(beta_), seg3h(dA_), cr_b, ALU.mult), ins=[dA_, crep[g]], outs=[beta_])
        w = NSH * T
        while w > T:
            w //= 2
            p.emit("vector", lambda e, w=w, beta_=beta_: e.tensor_tensor(
                beta_[:, 0:w], beta_[:, 0:w], beta_[:, w:2 * w], ALU.add),
                ins=[beta_], outs=[beta_])

    def stage_y_fwd(c, dt):
        yb = ystg[dt % 2]
        p.emit("vector", lambda e, yb=yb: e.tensor_tensor(
            yb[:], beta[0][:, 0:T], beta[1][:, 0:T], ALU.add),
            ins=[beta[0], beta[1]], outs=[yb])
        p.emit("vector", lambda e, yb=yb, dt=dt: e.tensor_tensor(
            yb[:], yb[:], ud[dt % 2][:], ALU.add), ins=[yb, ud[dt % 2]], outs=[yb])
        p.dma("gpsimd", lambda e, dt=dt, c=c, yb=yb: e.dma_start(
            y_sp.ap()[dt * 128:(dt + 1) * 128, c * T:(c + 1) * T], yb[:]),
            ins=[yb], outs=[y_sp], sem=f"q_st_y{dt % 2}", append=True)

    def yz_prefetch(c, dt):
        yfb, zb = yfl[dt % 4], zfl[dt % 4]
        ld(yfb[:], y_sp.ap()[dt * 128:(dt + 1) * 128, c * T:(c + 1) * T], [yfb],
           ins=[y_sp])
        ld(zb[:], z_sp.ap()[dt * 128:(dt + 1) * 128, c * T:(c + 1) * T], [zb],
           ins=[z_sp])

    def stage_y_bwd(c, dt):
        par = dt % 2
        yfb, zb = yfl[dt % 4], zfl[dt % 4]
        ys, sg, gs = ysum[par], sgbuf[par], gsil[par]
        p.emit("vector", lambda e, ys=ys: e.tensor_tensor(
            ys[:], beta[0][:, 0:T][:, ::-1], beta[1][:, 0:T][:, ::-1], ALU.add),
            ins=[beta[0], beta[1]], outs=[ys])
        p.emit("vector", lambda e, ys=ys, dt=dt: e.tensor_tensor(
            ys[:], ys[:], ud[dt % 2][:], ALU.add), ins=[ys, ud[dt % 2]], outs=[ys])
        p.emit("vector", lambda e, yfb=yfb, ys=ys: e.tensor_tensor(
            ys[:], ys[:], yfb[:], ALU.add), ins=[ys, yfb], outs=[ys])
        p.emit("scalar", lambda e, zb=zb, sg=sg: e.activation(
            sg[:], zb[:], AF.Exp, scale=-1.0), ins=[zb], outs=[sg])
        p.emit("scalar", lambda e, sg=sg: e.activation(
            sg[:], sg[:], AF.Ln, bias=1.0), ins=[sg], outs=[sg])
        p.emit("scalar", lambda e, sg=sg: e.activation(
            sg[:], sg[:], AF.Exp, scale=-1.0), ins=[sg], outs=[sg])
        p.emit("vector", lambda e, zb=zb, sg=sg, gs=gs: e.tensor_tensor(
            gs[:], zb[:], sg[:], ALU.mult), ins=[zb, sg], outs=[gs])
        p.emit("vector", lambda e, dt=dt, ys=ys, gs=gs: e.tensor_tensor(
            gated[:, dt * T:(dt + 1) * T], ys[:], gs[:], ALU.mult),
            ins=[ys, gs], outs=[gated])

    def stage_partial_prefetch(c):
        pb = x_tm[0]
        ld(pb[:].rearrange("p (i d) -> p i d", i=NT),
           part_sp.ap()[c * T:(c + 1) * T, :].rearrange("(i p) d -> p i d", p=128),
           [pb], ins=[part_sp])

    def stage_outproj_final(c):
        for mo in range(NKM):
            wb = woutstg[mo % 2]
            ld(wb[:].rearrange("p (kt m) -> p kt m", kt=NDT),
               wc_out.ap()[mo].rearrange("kt k m -> k kt m"), [wb], ins=[wc_out])
            pp = psA if mo % 2 == 0 else psB
            for ki in range(NDT):
                p.emit("tensor", lambda e, ki=ki, wb=wb, pp=pp: e.matmul(
                    pp[:, 0:T], wb[:, ki * 128:(ki + 1) * 128],
                    gated[:, ki * T:(ki + 1) * T], start=(ki == 0), stop=(ki == NDT - 1)),
                    ins=[wb, gated], outs=[pp])
            p.emit("scalar", lambda e, mo=mo, pp=pp: e.activation(
                attn_fm[:, mo * T:(mo + 1) * T], pp[:, 0:T], AF.Copy),
                ins=[pp], outs=[attn_fm])
        pb = x_tm[0]
        for o in range(NKM):
            for tt in range(NT):
                pp = psTA if (o * NT + tt) % 2 == 0 else psTB
                p.emit("tensor", lambda e, o=o, tt=tt, pp=pp: e.transpose(
                    pp[:, 0:128], attn_fm[:, o * T + tt * 128:o * T + tt * 128 + 128],
                    ident16[:]), ins=[attn_fm, ident16], outs=[pp])
                p.emit("vector", lambda e, o=o, tt=tt, pp=pp, pb=pb: e.tensor_tensor(
                    pb[:, tt * DM + o * 128:tt * DM + o * 128 + 128],
                    pp[:, 0:128],
                    pb[:, tt * DM + o * 128:tt * DM + o * 128 + 128],
                    ALU.add), ins=[pp, pb], outs=[pb])
        p.dma("gpsimd", lambda e, c=c, pb=pb: e.dma_start(
            out_h.ap()[c * T:(c + 1) * T, :].rearrange("(i p) d -> p i d", p=128),
            pb[:].rearrange("p (i d) -> p i d", i=NT)),
            ins=[pb], outs=[out_h], sem="q_out", append=True)

    def b1_pieces(c):
        last = c == NCH - 1
        return [
            lambda: (stage_load_x(c), stage_ln(c, x_tm[0])),
            lambda: stage_quant(c),
            lambda: stage_ffn_mm(c),
            lambda: stage_ffn_partial(c),
            lambda: transpose_tm_to_fm(xn16, xn_fm, psTB),
            lambda: stage_inproj(c, range(0, 8)),
            lambda: stage_inproj(c, range(8, 16)),
            lambda: stage_inproj(c, range(16, 24)),
            lambda: stage_inproj(c, range(24, 32)),
            lambda: stage_conv(0, c % 2, range(0, 8)),
            lambda: stage_conv(0, c % 2, range(8, 16)),
            lambda: stage_xdbl(0, c % 2),
            lambda: stage_brep(c % 2, 0),
            lambda: stage_brep(c % 2, 1),
            lambda: stage_delta(0, c % 2, range(NDT)),
            lambda: stage_xspill_halo(c, last),
        ]

    def b2_pieces(c):
        g = c % 2
        return [
            lambda: p.dma("sync", [
                (lambda e, dt=dt, c=c: e.dma_start(
                    x_fm[:, dt * (T + 3):dt * (T + 3) + T],
                    x_sp.ap()[dt * 128:(dt + 1) * 128, c * T:(c + 1) * T]))
                for dt in range(NDT)], ins=[x_sp], outs=[x_fm], sem="q_x_fm"),
            None,
            None,
            lambda: stage_conv(1, g, range(0, 8)),
            lambda: stage_conv(1, g, range(8, 16)),
            lambda: ([p.emit("vector", lambda e, dt=dt: e.tensor_copy(
                x_fm[:, dt * (T + 3) + T:dt * (T + 3) + T + 3],
                x_fm[:, dt * (T + 3):dt * (T + 3) + 3]),
                ins=[x_fm], outs=[x_fm]) for dt in range(NDT)]
                if c > 0 else None),
            lambda: stage_xdbl(1, g),
            lambda: stage_brep(g, 0),
            lambda: stage_brep(g, 1),
            lambda: stage_delta(1, g, range(0, 8)),
            lambda: stage_delta(1, g, range(8, 16)),
            None, None, None, None, None, None,
        ]

    p.emit("vector", lambda e: e.memset(carry[:], 0.0), outs=[carry])
    for dt in range(NDT):
        p.emit("vector", lambda e, dt=dt: e.memset(
            x_fm[:, dt * (T + 3):dt * (T + 3) + 3], 0.0), outs=[x_fm])
    for f in b1_pieces(0):
        f()
    NU = 2 * NDT
    scan_prep(0, 0, 0, 0, rev=False, gu=0)
    scan_prep(0, 0, 0, 1, rev=False, gu=1)
    for c in range(NCH):
        g = c % 2
        nxt = b1_pieces(c + 1) if c + 1 < NCH else [None] * NDT
        for u in range(NU):
            gu = c * NU + u
            if u + 2 < NU:
                scan_prep(c, 0, g, u + 2, rev=False, gu=gu + 2)
            elif c + 1 < NCH:
                scan_prep(c + 1, 0, (c + 1) % 2, u + 2 - NU, rev=False, gu=gu + 2)
            if u % 2 == 0 and nxt[u // 2] is not None:
                nxt[u // 2]()
            scan_heavy(c, 0, g, u, rev=False, gu=gu)
            if u % 2 == 1:
                stage_y_fwd(c, u // 2)

    p.emit("vector", lambda e: e.memset(carry[:], 0.0), outs=[carry])
    for dt in range(NDT):
        p.emit("vector", lambda e, dt=dt: e.memset(
            x_fm[:, dt * (T + 3) + T:dt * (T + 3) + T + 3], 0.0), outs=[x_fm])
    for f in b2_pieces(NCH - 1):
        if f is not None:
            f()
    scan_prep(NCH - 1, 1, (NCH - 1) % 2, 0, rev=True, gu=0)
    scan_prep(NCH - 1, 1, (NCH - 1) % 2, 1, rev=True, gu=1)
    for c in range(NCH - 1, -1, -1):
        g = c % 2
        nxt = b2_pieces(c - 1) if c > 0 else [None] * NDT
        if c == NCH - 1:
            for dtp in range(3):
                yz_prefetch(c, dtp)
        stage_partial_prefetch(c)
        for u in range(NU):
            gu = (NCH - 1 - c) * NU + u
            if u + 2 < NU:
                scan_prep(c, 1, g, u + 2, rev=True, gu=gu + 2)
            elif c > 0:
                scan_prep(c - 1, 1, (c - 1) % 2, u + 2 - NU, rev=True, gu=gu + 2)
            if u % 2 == 0:
                if nxt[u // 2] is not None:
                    nxt[u // 2]()
                if u // 2 + 3 < NDT:
                    yz_prefetch(c, u // 2 + 3)
                elif c > 0:
                    yz_prefetch(c - 1, u // 2 + 3 - NDT)
            scan_heavy(c, 1, g, u, rev=True, gu=gu)
            if u % 2 == 1:
                stage_y_bwd(c, u // 2)
        stage_outproj_final(c)

    with nc.Block() as block:
        @block.sync
        def _(e):
            for f in p.prog["sync"]:
                f(e)

        @block.tensor
        def _(e):
            for f in p.prog["tensor"]:
                f(e)

        @block.vector
        def _(e):
            for f in p.prog["vector"]:
                f(e)

        @block.scalar
        def _(e):
            for f in p.prog["scalar"]:
                f(e)

        @block.gpsimd
        def _(e):
            for f in p.prog["gpsimd"]:
                f(e)

    st.close()
    return nc


def host_inputs(core_inp, b):
    f = np.ascontiguousarray
    w_in = core_inp["L0_attn_in_proj_w"]
    w_out = core_inp["L0_attn_out_proj_w"]
    wq = core_inp["L0_ffn_weight_q"]
    return {
        "hx": f(core_inp["L0_attn_hidden_states"][b]),
        "w_in_t": f(w_in.reshape(32, 128, 8, 128).transpose(0, 2, 3, 1)),
        "w_out_t": f(w_out.reshape(8, 128, 16, 128).transpose(0, 2, 3, 1)),
        "wq_t": f(wq.reshape(8, 128, 8, 128).transpose(2, 0, 3, 1)),
        "xproj_t": f(np.stack([
            core_inp["L0_attn_x_proj_w"].reshape(96, 16, 128).transpose(1, 2, 0),
            core_inp["L0_attn_x_proj_b_w"].reshape(96, 16, 128).transpose(1, 2, 0)])),
        "dtw_t": f(np.stack([
            core_inp["L0_attn_dt_proj_w"].reshape(16, 128, 64).transpose(0, 2, 1),
            core_inp["L0_attn_dt_proj_b_w"].reshape(16, 128, 64).transpose(0, 2, 1)])),
        "wconv": f(np.stack([
            core_inp["L0_attn_conv1d_w"].reshape(16, 128, 4),
            core_inp["L0_attn_conv1d_b_w"].reshape(16, 128, 4)])),
        "cbias": f(np.stack([
            core_inp["L0_attn_conv1d_bias"].reshape(16, 128),
            core_inp["L0_attn_conv1d_b_bias"].reshape(16, 128)])),
        "dtbias": f(np.stack([
            core_inp["L0_attn_dt_proj_bias"].reshape(16, 128),
            core_inp["L0_attn_dt_proj_b_bias"].reshape(16, 128)])),
        "alog": f(np.stack([
            core_inp["L0_attn_A_log"].reshape(16, 128, 16),
            core_inp["L0_attn_A_b_log"].reshape(16, 128, 16)])),
        "dvec": f(np.stack([
            core_inp["L0_attn_D"].reshape(16, 128),
            core_inp["L0_attn_D_b"].reshape(16, 128)])),
        "wsc": f(core_inp["L0_ffn_weight_scale"].reshape(8, 128)),
        "fbias": f(core_inp["L0_ffn_bias"].reshape(8, 128)),
    }


def kernel(**inputs):
    nc = build_program()
    in_maps = [host_inputs(inputs, b) for b in range(8)]
    res = run_bass_kernel_spmd(nc, in_maps, core_ids=list(range(8)))
    return np.stack([np.asarray(r["out"]) for r in res.results], axis=0)


# version 18
# speedup vs baseline: 1.0292x; 1.0292x over previous
import numpy as np
from contextlib import ExitStack

import concourse.bass as bass
import concourse.mybir as mybir
from concourse.bass_utils import run_bass_kernel_spmd

F32 = mybir.dt.float32
F16 = mybir.dt.float16
I8 = mybir.dt.int8
ALU = mybir.AluOpType
AF = mybir.ActivationFunctionType

L, DM, DI, NST = 8192, 1024, 2048, 16
NSH = NST // 2
T = 256
NCH = L // T
NT = T // 128
NDT = DI // 128
NKM = DM // 128
GRID = F16
ENGS = ("sync", "tensor", "vector", "scalar", "gpsimd")


class P:

    def __init__(self, nc, st):
        self.nc = nc
        self.st = st
        self.sems = {}
        self.count = {}
        self.prog = {e: [] for e in ENGS}
        self.floor = {e: {} for e in ENGS}
        self.bufs = {}
        for e in ENGS:
            self._sem(e)

    def _sem(self, name):
        if name not in self.sems:
            self.sems[name] = self.st.enter_context(self.nc.semaphore(name))
            self.count[name] = 0
        return name

    def _b(self, h):
        b = self.bufs.get(id(h))
        if b is None:
            b = {"w": {}, "r": []}
            self.bufs[id(h)] = b
        return b

    def _waits(self, eng, ins, outs, war=True):
        waits = {}
        fl = self.floor[eng]

        def need(tag):
            if tag is None:
                return
            s, v = tag
            if fl.get(s, 0) >= v:
                return
            if waits.get(s, 0) < v:
                waits[s] = v

        for h in ins:
            for t in self._b(h)["w"].items():
                need(t)
        for h in outs:
            b = self._b(h)
            if war:
                for t in b["w"].items():
                    need(t)
                for r in b["r"]:
                    need(r)
        wl = sorted(waits.items())
        for s, v in wl:
            fl[s] = v
        return wl

    def _push(self, eng, f, wl, sem_name, inc):
        sems = self.sems

        def replay(e, wl=wl, f=f, sem=sems[sem_name], inc=inc):
            for s, v in wl:
                e.wait_ge(sems[s], v)
            f(e).then_inc(sem, inc)

        self.prog[eng].append(replay)

    def _settle(self, ins, outs, tag, append=False):
        for h in outs:
            b = self._b(h)
            if append:
                b["w"][tag[0]] = tag[1]
            else:
                b["w"] = {tag[0]: tag[1]}
                b["r"] = []
        for h in ins:
            self._b(h)["r"].append(tag)

    def emit(self, eng, f, ins=(), outs=()):
        wl = self._waits(eng, ins, outs)
        self.count[eng] += 1
        self._push(eng, f, wl, eng, 1)
        self._settle(ins, outs, (eng, self.count[eng]))

    def dma(self, eng, fs, ins=(), outs=(), sem=None, append=False):
        if not isinstance(fs, (list, tuple)):
            fs = [fs]
        sem = self._sem(sem)
        wl = self._waits(eng, ins, outs, war=not append)
        first = True
        for f in fs:
            self.count[sem] += 16
            self._push(eng, f, wl if first else [], sem, 16)
            first = False
        self._settle(ins, outs, (sem, self.count[sem]), append=append)

    def wait(self, eng, sem, val):
        sems = self.sems
        self.prog[eng].append(lambda e, s=sem, v=val: e.wait_ge(sems[s], v))


def build_program():
    nc = bass.Bass("TRN2", target_bir_lowering=False)

    hx = nc.dram_tensor("hx", [L, DM], F32, kind="ExternalInput")
    w_in_t = nc.dram_tensor("w_in_t", [32, NKM, 128, 128], F32, kind="ExternalInput")
    w_out_t = nc.dram_tensor("w_out_t", [8, NDT, 128, 128], F32, kind="ExternalInput")
    wq_t = nc.dram_tensor("wq_t", [NKM, NKM, 128, 128], I8, kind="ExternalInput")
    xproj_t = nc.dram_tensor("xproj_t", [2, NDT, 128, 96], F32, kind="ExternalInput")
    dtw_t = nc.dram_tensor("dtw_t", [2, NDT, 64, 128], F32, kind="ExternalInput")
    wconv = nc.dram_tensor("wconv", [2, NDT, 128, 4], F32, kind="ExternalInput")
    cbias = nc.dram_tensor("cbias", [2, NDT, 128], F32, kind="ExternalInput")
    dtbias = nc.dram_tensor("dtbias", [2, NDT, 128], F32, kind="ExternalInput")
    alog = nc.dram_tensor("alog", [2, NDT, 128, NST], F32, kind="ExternalInput")
    dvec = nc.dram_tensor("dvec", [2, NDT, 128], F32, kind="ExternalInput")
    wsc = nc.dram_tensor("wsc", [NKM, 128], F32, kind="ExternalInput")
    fbias = nc.dram_tensor("fbias", [NKM, 128], F32, kind="ExternalInput")
    out_h = nc.dram_tensor("out", [L, DM], F32, kind="ExternalOutput")
    wc_in = nc.dram_tensor("wc_in", [32, NKM, 128, 128], F16)
    wc_out = nc.dram_tensor("wc_out", [8, NDT, 128, 128], F16)
    wc_q = nc.dram_tensor("wc_q", [NKM, NKM, 128, 128], F16)
    x_sp = nc.dram_tensor("x_sp", [DI, L], F16)
    z_sp = nc.dram_tensor("z_sp", [DI, L], F16)
    y_sp = nc.dram_tensor("y_sp", [DI, L], F16)
    part_sp = nc.dram_tensor("part_sp", [L, DM], F32)
    diag_dram = nc.dram_tensor("diag_dram", [2, NDT, 4, 128, 128], F16)

    st = ExitStack()
    sb = lambda name, shape, dt: st.enter_context(nc.sbuf_tensor(name, shape, dt))
    psa = lambda name, shape, dt: st.enter_context(nc.psum_tensor(name, shape, dt))

    ident16 = sb("ident16", [128, 128], F16)
    ident32 = sb("ident32", [128, 128], F32)
    ones32 = sb("ones32", [1, 128], F32)
    xprojT = sb("xprojT", [128, 2 * NDT * 96], F16)
    dtwT = sb("dtwT", [64, 2 * NDT * 128], F16)
    wcv16 = sb("wcv16", [128, 2 * NDT * 4], F16)
    diag_stg = [sb(f"diag{i}", [128, 4 * 128], F16) for i in range(2)]
    diag_bld = diag_stg[0]
    acol = sb("acol", [128, 2 * NDT * NST], F32)
    dcol = sb("dcol", [128, 2 * NDT], F32)
    cbcol = sb("cbcol", [128, 2 * NDT], F32)
    dtbcol = sb("dtbcol", [128, 2 * NDT], F32)
    wscol = sb("wscol", [128, NKM], F32)
    fbcol = sb("fbcol", [128, NKM], F32)
    wstg32 = sb("wstg32", [128, 3072], F32)
    wstg_i8 = sb("wstg_i8", [128, NKM * 128], I8)
    winstg = [sb(f"winstg{i}", [128, NKM * 128], F16) for i in range(2)]
    woutstg = [sb(f"woutstg{i}", [128, NDT * 128], F16) for i in range(2)]

    x_tm = [sb("x_tm0", [128, NT * DM], F32)]
    lncol = sb("lncol", [128, 24], F32)
    xn_tm = sb("xn_tm", [128, NT * DM], F32)
    xn16 = sb("xn16", [128, NT * DM], F16)
    xq16 = sb("xq16", [128, NT * DM], F16)
    xn_fm = sb("xn_fm", [128, NKM * T], F16)
    xq_fm = sb("xq_fm", [128, NKM * T], F16)
    xsrep = sb("xsrep", [128, T], F32)
    ffn_fm = sb("ffn_fm", [128, NKM * T], F16)
    attn_fm = ffn_fm
    x_fm = sb("x_fm", [128, NDT * (T + 3)], F16)
    dall = [sb(f"dall{i}", [128, NDT * T], F16) for i in range(2)]
    zstg = [sb(f"zstg{i}", [128, T], F16) for i in range(2)]
    xc_fm = [sb(f"xc_fm{i}", [128, NDT * T], F16) for i in range(2)]
    dtin = [sb(f"dtin{i}", [64, T], F16) for i in range(2)]
    bcf = [sb(f"bcf{i}", [128, T], F16) for i in range(2)]
    sel = sb("sel", [128, 32 * 128], F16)
    brep = [sb(f"brep{i}", [128, NST * T], F16) for i in range(2)]
    crep = [sb(f"crep{i}", [128, NST * T], F16) for i in range(2)]
    dsc1 = sb("dsc1", [128, T], F32)
    dsc2 = sb("dsc2", [128, T], F32)
    _uc = sb("uclip", [128, T], F16)
    uclip = [_uc, _uc]
    du = [sb(f"du_{i}", [128, T], F16) for i in range(2)]
    ud = [sb(f"ud{i}", [128, T], F32) for i in range(2)]
    dA = [sb(f"dA{i}", [128, NSH * T], GRID) for i in range(3)]
    beta = [sb(f"beta{i}", [128, NSH * T], GRID) for i in range(2)]
    fixb = [sb(f"fixb{i}", [128, NSH], F32) for i in range(2)]
    carry = sb("carry", [128, NDT * NST], F16)
    ystg = [sb(f"ystg{i}", [128, T], F16) for i in range(2)]
    yfl = [sb(f"yfl{i}", [128, T], F16) for i in range(4)]
    zfl = [sb(f"zfl{i}", [128, T], F16) for i in range(4)]
    _ys = sb("ysum", [128, T], F32)
    _gs = sb("gsil", [128, T], F16)
    ysum = [_ys, _ys]
    gsil = [_gs, _gs]
    sgbuf = [sb(f"sgbuf{i}", [128, T], F32) for i in range(2)]
    gated = sb("gated", [128, NDT * T], F16)

    psA = psa("psA", [128, 512], F32)
    psB = psa("psB", [128, 512], F32)
    psC = psa("psC", [128, 512], F32)
    psD = psa("psD", [128, 512], F32)
    psE = psa("psE", [128, 512], F32)
    psF = psa("psF", [128, 512], F32)
    psTA = psa("psTA", [128, 1024], F16)
    psTB = psa("psTB", [128, 1024], F16)

    p = P(nc, st)

    def seg3(t, n=NST):
        return t[:].rearrange("p (n t) -> p n t", n=n)

    def ld(dst_ap, src_ap, outs, slow=False, ins=()):
        p.dma("sync", lambda e, d=dst_ap, s=src_ap, sl=slow: e.dma_start(
            d, s, allow_slow_non_contiguous=sl), ins=ins, outs=outs,
            sem="q_" + outs[0].name)

    p.emit("gpsimd", lambda e: e.memset(ident16[:], 1.0), outs=[ident16])
    p.emit("gpsimd", lambda e: e.affine_select(
        ident16[:], ident16[:], [[-1, 128]], ALU.is_equal, 0.0, base=0,
        channel_multiplier=1), ins=[ident16], outs=[ident16])
    p.emit("gpsimd", lambda e: e.memset(ident32[:], 1.0), outs=[ident32])
    p.emit("gpsimd", lambda e: e.affine_select(
        ident32[:], ident32[:], [[-1, 128]], ALU.is_equal, 0.0, base=0,
        channel_multiplier=1), ins=[ident32], outs=[ident32])
    p.emit("gpsimd", lambda e: e.memset(ones32[:], 1.0), outs=[ones32])

    ld(wstg32[:, 0:2 * NDT * 4].rearrange("p (r dt k) -> p r dt k", r=2, dt=NDT),
       wconv.ap().rearrange("r dt p k -> p r dt k"), [wstg32])
    p.emit("vector", lambda e: e.tensor_copy(wcv16[:], wstg32[:, 0:2 * NDT * 4]),
           ins=[wstg32], outs=[wcv16])
    ld(cbcol[:].rearrange("p (r dt) -> p r dt", r=2),
       cbias.ap().rearrange("r dt p -> p r dt"), slow=True, outs=[cbcol])
    ld(dtbcol[:].rearrange("p (r dt) -> p r dt", r=2),
       dtbias.ap().rearrange("r dt p -> p r dt"), slow=True, outs=[dtbcol])
    ld(dcol[:].rearrange("p (r dt) -> p r dt", r=2),
       dvec.ap().rearrange("r dt p -> p r dt"), slow=True, outs=[dcol])
    ld(wscol[:], wsc.ap().rearrange("m p -> p m"), slow=True, outs=[wscol])
    ld(fbcol[:], fbias.ap().rearrange("m p -> p m"), slow=True, outs=[fbcol])
    ld(acol[:].rearrange("p (r dt n) -> p r dt n", r=2, dt=NDT),
       alog.ap().rearrange("r dt p n -> p r dt n"), [acol])
    p.emit("scalar", lambda e: e.activation(acol[:], acol[:], AF.Exp),
           ins=[acol], outs=[acol])
    p.emit("vector", lambda e: e.tensor_scalar(acol[:], acol[:], -1.0, None, ALU.mult),
           ins=[acol], outs=[acol])
    for r in range(2):
        ld(wstg32[:, 0:NDT * 96].rearrange("p (dt m) -> p dt m", dt=NDT),
           xproj_t.ap()[r].rearrange("dt k m -> k dt m"), [wstg32])
        p.emit("vector", lambda e, r=r: e.tensor_copy(
            xprojT[:, r * NDT * 96:(r + 1) * NDT * 96], wstg32[:, 0:NDT * 96]),
            ins=[wstg32], outs=[xprojT])
        ld(wstg32[0:64, 0:NDT * 128].rearrange("p (dt m) -> p dt m", dt=NDT),
           dtw_t.ap()[r].rearrange("dt k m -> k dt m"), [wstg32])
        p.emit("vector", lambda e, r=r: e.tensor_copy(
            dtwT[:, r * NDT * 128:(r + 1) * NDT * 128],
            wstg32[0:64, 0:NDT * 128]), ins=[wstg32], outs=[dtwT])

    for mt in range(32):
        ld(wstg32[:, 0:NKM * 128].rearrange("p (kt m) -> p kt m", kt=NKM),
           w_in_t.ap()[mt].rearrange("kt k m -> k kt m"), [wstg32])
        p.emit("vector", lambda e: e.tensor_copy(
            winstg[0][:], wstg32[:, 0:NKM * 128]), ins=[wstg32], outs=[winstg[0]])
        p.dma("gpsimd", lambda e, mt=mt: e.dma_start(
            wc_in.ap()[mt].rearrange("kt k m -> k kt m"),
            winstg[0][:].rearrange("p (kt m) -> p kt m", kt=NKM)),
            ins=[winstg[0]], outs=[wc_in], sem="q_wc_in", append=True)
    for mo in range(8):
        for half in range(2):
            ld(wstg32[:, 0:NKM * 128].rearrange("p (a m) -> p a m", a=NKM),
               w_out_t.ap()[mo, half * NKM:(half + 1) * NKM].rearrange(
                   "kt k m -> k kt m"), [wstg32])
            p.emit("vector", lambda e: e.tensor_copy(
                winstg[0][:], wstg32[:, 0:NKM * 128]), ins=[wstg32], outs=[winstg[0]])
            p.dma("gpsimd", lambda e, mo=mo, half=half: e.dma_start(
                wc_out.ap()[mo, half * NKM:(half + 1) * NKM].rearrange(
                    "kt k m -> k kt m"),
                winstg[0][:].rearrange("p (kt m) -> p kt m", kt=NKM)),
                ins=[winstg[0]], outs=[wc_out], sem="q_wc_out", append=True)
    for ki in range(NKM):
        ld(wstg_i8[:].rearrange("p (mo m) -> p mo m", mo=NKM),
           wq_t.ap()[ki].rearrange("mo k m -> k mo m"), [wstg_i8])
        p.emit("vector", lambda e: e.tensor_copy(
            winstg[0][:], wstg_i8[:]),
            ins=[wstg_i8], outs=[winstg[0]])
        p.dma("gpsimd", lambda e, ki=ki: e.dma_start(
            wc_q.ap()[ki].rearrange("mo k m -> k mo m"),
            winstg[0][:].rearrange("p (mo m) -> p mo m", mo=NKM)),
            ins=[winstg[0]], outs=[wc_q], sem="q_wc_q", append=True)

    p.emit("gpsimd", lambda e: e.memset(sel[:], 1.0), outs=[sel])
    for n in range(32):
        p.emit("gpsimd", lambda e, n=n: e.affine_select(
            sel[64:96, n * 128:(n + 1) * 128], sel[64:96, n * 128:(n + 1) * 128],
            [[0, 128]], ALU.is_equal, 0.0, base=-n, channel_multiplier=1),
            ins=[sel], outs=[sel])

    def build_diag(r):
        for dt in range(NDT):
            for k in range(4):
                src_k = k if r == 0 else 3 - k
                wslice = wcv16[:, (r * NDT + dt) * 4 + src_k:
                               (r * NDT + dt) * 4 + src_k + 1]
                dst = diag_bld[:, k * 128:(k + 1) * 128]
                p.emit("gpsimd", lambda e, d=dst, w=wslice: e.affine_select(
                    d, w.broadcast_to((128, 128)), [[-1, 128]], ALU.is_equal, 0.0,
                    base=0, channel_multiplier=1), ins=[wcv16], outs=[diag_bld])
            p.dma("gpsimd", lambda e, r=r, dt=dt: e.dma_start(
                diag_dram.ap()[r, dt].rearrange("k p m -> p k m"),
                diag_bld[:].rearrange("p (k m) -> p k m", k=4)),
                ins=[diag_bld], outs=[diag_dram], sem="q_diag", append=True)

    build_diag(0)
    build_diag(1)

    build_diag(1)

    def stage_load_x(c):
        xb = x_tm[0]
        ld(xb[:].rearrange("p (i d) -> p i d", i=NT),
           hx.ap()[c * T:(c + 1) * T, :].rearrange("(i p) d -> p i d", p=128), [xb])
        return xb

    def stage_ln(c, xb):
        scr = wstg32[:, 2048:3072]
        for tt in range(NT):
            xs = xb[:, tt * DM:(tt + 1) * DM]
            p.emit("scalar", lambda e, xs=xs, scr=scr: e.activation(
                scr, xs, AF.Copy, accum_out=lncol[:, 0:1]),
                ins=[xb], outs=[wstg32, lncol])
            p.emit("scalar", lambda e, xs=xs, scr=scr: e.activation(
                scr, xs, AF.Square, accum_out=lncol[:, 1:2]),
                ins=[xb], outs=[wstg32, lncol])
            p.emit("vector", lambda e: e.tensor_scalar(
                lncol[:, 2:3], lncol[:, 0:1], 1.0 / DM, None, ALU.mult),
                ins=[lncol], outs=[lncol])
            p.emit("vector", lambda e: e.tensor_scalar(
                lncol[:, 3:4], lncol[:, 1:2], 1.0 / DM, None, ALU.mult),
                ins=[lncol], outs=[lncol])
            p.emit("vector", lambda e: e.tensor_tensor(
                lncol[:, 4:5], lncol[:, 2:3], lncol[:, 2:3], ALU.mult),
                ins=[lncol], outs=[lncol])
            p.emit("vector", lambda e: e.tensor_tensor(
                lncol[:, 5:6], lncol[:, 3:4], lncol[:, 4:5], ALU.subtract),
                ins=[lncol], outs=[lncol])
            p.emit("vector", lambda e: e.tensor_scalar(
                lncol[:, 5:6], lncol[:, 5:6], 1e-5, None, ALU.add),
                ins=[lncol], outs=[lncol])
            p.emit("scalar", lambda e: e.activation(
                lncol[:, 6:7], lncol[:, 5:6], AF.Ln), ins=[lncol], outs=[lncol])
            p.emit("scalar", lambda e: e.activation(
                lncol[:, 7:8], lncol[:, 6:7], AF.Exp, scale=-0.5),
                ins=[lncol], outs=[lncol])
            p.emit("vector", lambda e: e.tensor_tensor(
                lncol[:, 10:11], lncol[:, 2:3], lncol[:, 7:8], ALU.mult),
                ins=[lncol], outs=[lncol])
            p.emit("vector", lambda e: e.tensor_scalar(
                lncol[:, 10:11], lncol[:, 10:11], -1.0, None, ALU.mult),
                ins=[lncol], outs=[lncol])
            p.emit("vector", lambda e, xs=xs, tt=tt: e.tensor_scalar(
                xn_tm[:, tt * DM:(tt + 1) * DM], xs, lncol[:, 7:8], lncol[:, 10:11],
                ALU.mult, ALU.add), ins=[xb, lncol], outs=[xn_tm])
        p.emit("vector", lambda e: e.tensor_copy(xn16[:], xn_tm[:]),
               ins=[xn_tm], outs=[xn16])

    def stage_quant(c):
        for tt in range(NT):
            xs = xn_tm[:, tt * DM:(tt + 1) * DM]
            q1s = wstg32[:, 0:DM]
            q2s = wstg32[:, DM:2 * DM]
            p.emit("vector", lambda e, xs=xs: e.tensor_reduce(
                lncol[:, 12:13], xs, mybir.AxisListType.X, ALU.max,
                apply_absolute_value=True), ins=[xn_tm], outs=[lncol])
            p.emit("vector", lambda e: e.tensor_scalar(
                lncol[:, 13:14], lncol[:, 12:13], 1e-6, 1.0 / 127.0, ALU.max, ALU.mult),
                ins=[lncol], outs=[lncol])
            p.emit("vector", lambda e: e.reciprocal(lncol[:, 14:15], lncol[:, 13:14]),
                   ins=[lncol], outs=[lncol])
            p.emit("vector", lambda e, tt=tt: e.tensor_copy(
                lncol[:, 16 + tt:17 + tt], lncol[:, 13:14]), ins=[lncol], outs=[lncol])
            p.emit("vector", lambda e, xs=xs, q1s=q1s: e.tensor_scalar(
                q1s, xs, lncol[:, 14:15], None, ALU.mult),
                ins=[xn_tm, lncol], outs=[wstg32])
            p.emit("vector", lambda e, q1s=q1s: e.tensor_scalar(
                q1s, q1s, -128.0, 127.0, ALU.max, ALU.min), ins=[wstg32], outs=[wstg32])
            p.emit("vector", lambda e, q1s=q1s: e.tensor_scalar(
                q1s, q1s, 128.5, None, ALU.add), ins=[wstg32], outs=[wstg32])
            p.emit("vector", lambda e, q1s=q1s, q2s=q2s: e.tensor_copy(
                q2s, q1s), ins=[wstg32], outs=[wstg32])
            q3s = wstg32[:, 2 * DM:3 * DM]
            for bit in (128.0, 64.0, 32.0, 16.0, 8.0, 4.0, 2.0, 1.0):
                p.emit("vector", lambda e, q2s=q2s, q3s=q3s, b=bit: e.tensor_scalar(
                    q3s, q2s, b, -b, ALU.is_ge, ALU.mult), ins=[wstg32], outs=[wstg32])
                p.emit("vector", lambda e, q2s=q2s, q3s=q3s: e.tensor_tensor(
                    q2s, q2s, q3s, ALU.add), ins=[wstg32], outs=[wstg32])
            p.emit("vector", lambda e, q1s=q1s, q2s=q2s: e.scalar_tensor_tensor(
                q1s, q2s, -1.0, q1s, ALU.mult, ALU.add), ins=[wstg32], outs=[wstg32])
            p.emit("vector", lambda e, q1s=q1s, tt=tt: e.tensor_scalar(
                xq16[:, tt * DM:(tt + 1) * DM], q1s, -128.0, None, ALU.add),
                ins=[wstg32], outs=[xq16])
        p.dma("sync", [
            (lambda e, tt=tt: e.dma_start(
                wstg32[0:1, 2048 + tt * 128:2048 + (tt + 1) * 128], lncol[:, 16 + tt:17 + tt],
                allow_slow_non_contiguous=True))
            for tt in range(NT)], ins=[lncol], outs=[wstg32], sem="q_xs_row")
        p.emit("tensor", lambda e: e.matmul(
            psF[:, 0:T], ones32[:], wstg32[0:1, 2048:2048 + T], start=True, stop=True),
            ins=[ones32, wstg32], outs=[psF])
        p.emit("vector", lambda e: e.tensor_copy(xsrep[:], psF[:, 0:T]),
               ins=[psF], outs=[xsrep])

    def transpose_tm_to_fm(src16, dst16, pst):
        for kt in range(NKM):
            for tt in range(NT):
                slot = ((kt * NT + tt) % 8) * 128
                p.emit("tensor", lambda e, kt=kt, tt=tt, slot=slot, s=src16, pp=pst:
                       e.transpose(pp[:, slot:slot + 128],
                                   s[:, tt * DM + kt * 128:tt * DM + kt * 128 + 128],
                                   ident16[:]), ins=[src16, ident16], outs=[pst])
                p.emit("vector", lambda e, kt=kt, tt=tt, slot=slot, d=dst16, pp=pst:
                       e.tensor_copy(d[:, kt * T + tt * 128:kt * T + tt * 128 + 128],
                                     pp[:, slot:slot + 128]), ins=[pst], outs=[dst16])

    def stage_ffn_mm(c):
        transpose_tm_to_fm(xq16, xq_fm, psTA)
        for mo in range(NKM):
            wb = winstg[mo % 2]
            ld(wb[:].rearrange("p (ki m) -> p ki m", ki=NKM),
               wc_q.ap()[:, mo].rearrange("ki k m -> k ki m"), [wb], ins=[wc_q])
            pp = psA if mo % 2 == 0 else psB
            for ki in range(NKM):
                p.emit("tensor", lambda e, ki=ki, wb=wb, pp=pp: e.matmul(
                    pp[:, 0:T], wb[:, ki * 128:(ki + 1) * 128],
                    xq_fm[:, ki * T:(ki + 1) * T], start=(ki == 0), stop=(ki == NKM - 1)),
                    ins=[wb, xq_fm], outs=[pp])
            p.emit("vector", lambda e, mo=mo, pp=pp: e.tensor_tensor(
                ffn_fm[:, mo * T:(mo + 1) * T], pp[:, 0:T], xsrep[:], ALU.mult),
                ins=[pp, xsrep], outs=[ffn_fm])
            p.emit("scalar", lambda e, mo=mo: e.activation(
                ffn_fm[:, mo * T:(mo + 1) * T], ffn_fm[:, mo * T:(mo + 1) * T],
                AF.Identity, bias=fbcol[:, mo:mo + 1], scale=wscol[:, mo:mo + 1]),
                ins=[ffn_fm, wscol, fbcol], outs=[ffn_fm])

    def stage_ffn_partial(c):
        xb = x_tm[0]
        for o in range(NKM):
            for tt in range(NT):
                pp = psTA if (o * NT + tt) % 2 == 0 else psTB
                p.emit("tensor", lambda e, o=o, tt=tt, pp=pp: e.transpose(
                    pp[:, 0:128], ffn_fm[:, o * T + tt * 128:o * T + tt * 128 + 128],
                    ident16[:]), ins=[ffn_fm, ident16], outs=[pp])
                p.emit("vector", lambda e, o=o, tt=tt, pp=pp, xb=xb: e.tensor_tensor(
                    xb[:, tt * DM + o * 128:tt * DM + o * 128 + 128],
                    pp[:, 0:128], xb[:, tt * DM + o * 128:tt * DM + o * 128 + 128],
                    ALU.add), ins=[pp, xb], outs=[xb])
        p.dma("gpsimd", lambda e, c=c, xb=xb: e.dma_start(
            part_sp.ap()[c * T:(c + 1) * T, :].rearrange("(i p) d -> p i d", p=128),
            xb[:].rearrange("p (i d) -> p i d", i=NT)),
            ins=[xb], outs=[part_sp], sem="q_part_sp", append=True)

    def stage_inproj(c, mts):
        for mt in mts:
            wb = winstg[mt % 2]
            ld(wb[:].rearrange("p (kt m) -> p kt m", kt=NKM),
               wc_in.ap()[mt].rearrange("kt k m -> k kt m"), [wb], ins=[wc_in])
            pp = psA if mt % 2 == 0 else psB
            for kt in range(NKM):
                p.emit("tensor", lambda e, kt=kt, wb=wb, pp=pp: e.matmul(
                    pp[:, 0:T], wb[:, kt * 128:(kt + 1) * 128],
                    xn_fm[:, kt * T:(kt + 1) * T], start=(kt == 0), stop=(kt == NKM - 1)),
                    ins=[wb, xn_fm], outs=[pp])
            if mt < NDT:
                dst = x_fm[:, mt * (T + 3) + 3: mt * (T + 3) + 3 + T]
                p.emit("scalar", lambda e, dst=dst, pp=pp: e.activation(
                    dst, pp[:, 0:T], AF.Copy), ins=[pp], outs=[x_fm])
            else:
                zb = zstg[mt % 2]
                p.emit("scalar", lambda e, zb=zb, pp=pp: e.activation(
                    zb[:], pp[:, 0:T], AF.Copy), ins=[pp], outs=[zb])
                p.dma("gpsimd", lambda e, mt=mt, c=c, zb=zb: e.dma_start(
                    z_sp.ap()[(mt - NDT) * 128:(mt - NDT + 1) * 128, c * T:(c + 1) * T],
                    zb[:]), ins=[zb], outs=[z_sp], sem=f"q_st_z{mt % 2}", append=True)

    def stage_xspill_halo(c, last):
        p.dma("gpsimd", [
            (lambda e, mt=mt, c=c: e.dma_start(
                x_sp.ap()[mt * 128:(mt + 1) * 128, c * T:(c + 1) * T],
                x_fm[:, mt * (T + 3) + 3: mt * (T + 3) + 3 + T]))
            for mt in range(NDT)], ins=[x_fm], outs=[x_sp], sem="q_st_x")
        if not last:
            for dt in range(NDT):
                p.emit("vector", lambda e, dt=dt: e.tensor_copy(
                    x_fm[:, dt * (T + 3):dt * (T + 3) + 3],
                    x_fm[:, dt * (T + 3) + T:dt * (T + 3) + T + 3]),
                    ins=[x_fm], outs=[x_fm])

    def stage_conv(r, g, dts):
        dts = list(dts)
        ld(diag_stg[dts[0] % 2][:].rearrange("p (k m) -> p k m", k=4),
           diag_dram.ap()[r, dts[0]].rearrange("k p m -> p k m"),
           [diag_stg[dts[0] % 2]], ins=[diag_dram])
        for i, dt in enumerate(dts):
            db = diag_stg[dt % 2]
            if i + 1 < len(dts):
                nxt_dt = dts[i + 1]
                ld(diag_stg[nxt_dt % 2][:].rearrange("p (k m) -> p k m", k=4),
                   diag_dram.ap()[r, nxt_dt].rearrange("k p m -> p k m"),
                   [diag_stg[nxt_dt % 2]], ins=[diag_dram])
            pp = psC if dt % 2 == 0 else psD
            for k in range(4):
                p.emit("tensor", lambda e, dt=dt, k=k, pp=pp, db=db: e.matmul(
                    pp[:, 0:T], db[:, k * 128:(k + 1) * 128],
                    x_fm[:, dt * (T + 3) + k: dt * (T + 3) + k + T],
                    start=(k == 0), stop=(k == 3)), ins=[db, x_fm], outs=[pp])
            p.emit("scalar", lambda e, dt=dt, r=r, pp=pp, g=g: e.activation(
                xc_fm[g][:, dt * T:(dt + 1) * T], pp[:, 0:T], AF.Identity,
                bias=cbcol[:, r * NDT + dt:r * NDT + dt + 1]),
                ins=[pp, cbcol], outs=[xc_fm[g]])

    def stage_xdbl(r, g):
        for kt in range(NDT):
            p.emit("tensor", lambda e, kt=kt, r=r, g=g: e.matmul(
                psC[0:96, 0:T], xprojT[:, (r * NDT + kt) * 96:(r * NDT + kt + 1) * 96],
                xc_fm[g][:, kt * T:(kt + 1) * T], start=(kt == 0),
                stop=(kt == NDT - 1)), ins=[xprojT, xc_fm[g]], outs=[psC])
        p.emit("scalar", lambda e, g=g: e.activation(
            dtin[g][:], psC[0:64, 0:T], AF.Copy), ins=[psC], outs=[dtin[g]])
        p.emit("scalar", lambda e, g=g: e.activation(
            bcf[g][64:96, :], psC[64:96, 0:T], AF.Copy), ins=[psC], outs=[bcf[g]])

    def stage_brep(g, which):
        rep = brep[g] if which == 0 else crep[g]
        base = 0 if which == 0 else 16
        for n in range(NST):
            pp = psE if n % 2 == 0 else psF
            p.emit("tensor", lambda e, n=n, pp=pp, g=g: e.matmul(
                pp[:, 0:T], sel[64:96, (base + n) * 128:(base + n + 1) * 128],
                bcf[g][64:96, :], start=True, stop=True),
                ins=[sel, bcf[g]], outs=[pp])
            if n % 2 == 0:
                p.emit("vector", lambda e, n=n, pp=pp, rep=rep: e.tensor_copy(
                    rep[:, n * T:(n + 1) * T], pp[:, 0:T]), ins=[pp], outs=[rep])
            else:
                p.emit("scalar", lambda e, n=n, pp=pp, rep=rep: e.activation(
                    rep[:, n * T:(n + 1) * T], pp[:, 0:T], AF.Copy),
                    ins=[pp], outs=[rep])

    def stage_delta(r, g, dts):
        for dt in dts:
            idx = r * NDT + dt
            pp = psC if dt % 2 == 0 else psD
            p.emit("tensor", lambda e, idx=idx, pp=pp, g=g: e.matmul(
                pp[:, 0:T], dtwT[:, idx * 128:(idx + 1) * 128], dtin[g][:],
                start=True, stop=True), ins=[dtwT, dtin[g]], outs=[pp])
            p.emit("vector", lambda e, idx=idx, pp=pp: e.tensor_scalar(
                dsc1[:], pp[:, 0:T], dtbcol[:, idx:idx + 1], 1e-5, ALU.add, ALU.max),
                ins=[pp, dtbcol], outs=[dsc1])
            p.emit("vector", lambda e, idx=idx: e.tensor_scalar(
                dsc2[:], dsc1[:], 1.0, dtbcol[:, idx:idx + 1], ALU.min, ALU.add),
                ins=[dsc1, dtbcol], outs=[dsc2])
            p.emit("scalar", lambda e: e.activation(dsc1[:], dsc2[:], AF.Exp),
                   ins=[dsc2], outs=[dsc1])
            p.emit("scalar", lambda e, dt=dt, g=g: e.activation(
                dall[g][:, dt * T:(dt + 1) * T], dsc1[:], AF.Ln, bias=1.0),
                ins=[dsc1], outs=[dall[g]])

    def scan_prep(c, r, g, u, rev, gu):
        dt, h = u // 2, u % 2
        idx = r * NDT + dt
        dA_ = dA[gu % 3]
        dslice = dall[g][:, dt * T:(dt + 1) * T]
        if h == 0:
            du_, ud_, uclip_ = du[dt % 2], ud[dt % 2], uclip[dt % 2]
            p.emit("vector", lambda e, dt=dt, uclip_=uclip_, g=g: e.tensor_scalar(
                uclip_[:], xc_fm[g][:, dt * T:(dt + 1) * T], 10.0, -10.0,
                ALU.min, ALU.max), ins=[xc_fm[g]], outs=[uclip_])
            p.emit("vector", lambda e, du_=du_, ds=dslice, uclip_=uclip_, g=g:
                   e.tensor_tensor(du_[:], ds, uclip_[:], ALU.mult),
                   ins=[dall[g], uclip_], outs=[du_])
            p.emit("vector", lambda e, idx=idx, ud_=ud_, uclip_=uclip_:
                   e.tensor_scalar(ud_[:], uclip_[:], dcol[:, idx:idx + 1], None,
                                   ALU.mult), ins=[uclip_, dcol], outs=[ud_])
        din = dslice[:, ::-1] if rev else dslice
        for n in range(NSH):
            na = h * NSH + n
            p.emit("scalar", lambda e, n=n, na=na, idx=idx, din=din, dA_=dA_, g=g:
                   e.activation(dA_[:, n * T:(n + 1) * T], din, AF.Exp,
                                scale=acol[:, idx * NST + na:idx * NST + na + 1]),
                   ins=[dall[g], acol], outs=[dA_])

    def seg3h(t):
        return t[:].rearrange("p (n t) -> p n t", n=NSH)

    def scan_heavy(c, r, g, u, rev, gu):
        dt, h = u // 2, u % 2
        par = u % 2
        dA_, beta_ = dA[gu % 3], beta[par]
        du16_ = du[dt % 2]
        fx = fixb[par]
        bslice = brep[g][:, h * NSH * T:(h + 1) * NSH * T]
        cslice = crep[g][:, h * NSH * T:(h + 1) * NSH * T]
        b3 = bslice.rearrange("p (n t) -> p n t", n=NSH)
        c3 = cslice.rearrange("p (n t) -> p n t", n=NSH)
        if rev:
            du_b = du16_[:, ::-1].unsqueeze(1).broadcast_to((128, NSH, T))
            br_b = b3[:, :, ::-1]
        else:
            du_b = du16_[:].unsqueeze(1).broadcast_to((128, NSH, T))
            br_b = b3
        p.emit("vector", lambda e, du_b=du_b, br_b=br_b, beta_=beta_: e.tensor_tensor(
            seg3h(beta_), du_b, br_b, ALU.mult), ins=[du16_, brep[g]], outs=[beta_])
        cs = carry[:, dt * NST + h * NSH: dt * NST + (h + 1) * NSH]
        p.emit("vector", lambda e, cs=cs, dA_=dA_, fx=fx: e.tensor_tensor(
            fx[:].rearrange("p (n o) -> p n o", o=1),
            seg3h(dA_)[:, :, 0:1], cs.rearrange("p (n o) -> p n o", o=1), ALU.mult),
            ins=[dA_, carry], outs=[fx])
        p.emit("vector", lambda e, beta_=beta_, fx=fx: e.tensor_tensor(
            seg3h(beta_)[:, :, 0:1], seg3h(beta_)[:, :, 0:1],
            fx[:].rearrange("p (n o) -> p n o", o=1), ALU.add),
            ins=[beta_, fx], outs=[beta_])
        p.emit("vector", lambda e, dA_=dA_: e.memset(seg3h(dA_)[:, :, 0:1], 0.0),
               ins=[dA_], outs=[dA_])
        p.emit("vector", lambda e, dA_=dA_, beta_=beta_: e.tensor_tensor_scan(
            dA_[:], dA_[:], beta_[:], 0.0, ALU.mult, ALU.add),
            ins=[dA_, beta_], outs=[dA_])
        p.emit("vector", lambda e, cs=cs, dA_=dA_: e.tensor_copy(
            cs.rearrange("p (n o) -> p n o", o=1), seg3h(dA_)[:, :, T - 1:T]),
            ins=[dA_], outs=[carry])
        cr_b = c3[:, :, ::-1] if rev else c3
        p.emit("vector", lambda e, cr_b=cr_b, dA_=dA_, beta_=beta_: e.tensor_tensor(
            seg3h(beta_), seg3h(dA_), cr_b, ALU.mult), ins=[dA_, crep[g]], outs=[beta_])
        w = NSH * T
        while w > T:
            w //= 2
            p.emit("vector", lambda e, w=w, beta_=beta_: e.tensor_tensor(
                beta_[:, 0:w], beta_[:, 0:w], beta_[:, w:2 * w], ALU.add),
                ins=[beta_], outs=[beta_])

    def stage_y_fwd(c, dt):
        yb = ystg[dt % 2]
        p.emit("vector", lambda e, yb=yb: e.tensor_tensor(
            yb[:], beta[0][:, 0:T], beta[1][:, 0:T], ALU.add),
            ins=[beta[0], beta[1]], outs=[yb])
        p.emit("vector", lambda e, yb=yb, dt=dt: e.tensor_tensor(
            yb[:], yb[:], ud[dt % 2][:], ALU.add), ins=[yb, ud[dt % 2]], outs=[yb])
        p.dma("gpsimd", lambda e, dt=dt, c=c, yb=yb: e.dma_start(
            y_sp.ap()[dt * 128:(dt + 1) * 128, c * T:(c + 1) * T], yb[:]),
            ins=[yb], outs=[y_sp], sem=f"q_st_y{dt % 2}", append=True)

    def yz_prefetch(c, dt):
        yfb, zb = yfl[dt % 4], zfl[dt % 4]
        ld(yfb[:], y_sp.ap()[dt * 128:(dt + 1) * 128, c * T:(c + 1) * T], [yfb],
           ins=[y_sp])
        ld(zb[:], z_sp.ap()[dt * 128:(dt + 1) * 128, c * T:(c + 1) * T], [zb],
           ins=[z_sp])

    def stage_y_bwd(c, dt):
        par = dt % 2
        yfb, zb = yfl[dt % 4], zfl[dt % 4]
        ys, sg, gs = ysum[par], sgbuf[par], gsil[par]
        p.emit("vector", lambda e, ys=ys: e.tensor_tensor(
            ys[:], beta[0][:, 0:T][:, ::-1], beta[1][:, 0:T][:, ::-1], ALU.add),
            ins=[beta[0], beta[1]], outs=[ys])
        p.emit("vector", lambda e, ys=ys, dt=dt: e.tensor_tensor(
            ys[:], ys[:], ud[dt % 2][:], ALU.add), ins=[ys, ud[dt % 2]], outs=[ys])
        p.emit("vector", lambda e, yfb=yfb, ys=ys: e.tensor_tensor(
            ys[:], ys[:], yfb[:], ALU.add), ins=[ys, yfb], outs=[ys])
        p.emit("scalar", lambda e, zb=zb, sg=sg: e.activation(
            sg[:], zb[:], AF.Exp, scale=-1.0), ins=[zb], outs=[sg])
        p.emit("scalar", lambda e, sg=sg: e.activation(
            sg[:], sg[:], AF.Ln, bias=1.0), ins=[sg], outs=[sg])
        p.emit("scalar", lambda e, sg=sg: e.activation(
            sg[:], sg[:], AF.Exp, scale=-1.0), ins=[sg], outs=[sg])
        p.emit("vector", lambda e, zb=zb, sg=sg, gs=gs: e.tensor_tensor(
            gs[:], zb[:], sg[:], ALU.mult), ins=[zb, sg], outs=[gs])
        p.emit("vector", lambda e, dt=dt, ys=ys, gs=gs: e.tensor_tensor(
            gated[:, dt * T:(dt + 1) * T], ys[:], gs[:], ALU.mult),
            ins=[ys, gs], outs=[gated])

    def stage_partial_prefetch(c):
        pb = x_tm[0]
        ld(pb[:].rearrange("p (i d) -> p i d", i=NT),
           part_sp.ap()[c * T:(c + 1) * T, :].rearrange("(i p) d -> p i d", p=128),
           [pb], ins=[part_sp])

    def stage_outproj_final(c):
        for mo in range(NKM):
            wb = woutstg[mo % 2]
            ld(wb[:].rearrange("p (kt m) -> p kt m", kt=NDT),
               wc_out.ap()[mo].rearrange("kt k m -> k kt m"), [wb], ins=[wc_out])
            pp = psA if mo % 2 == 0 else psB
            for ki in range(NDT):
                p.emit("tensor", lambda e, ki=ki, wb=wb, pp=pp: e.matmul(
                    pp[:, 0:T], wb[:, ki * 128:(ki + 1) * 128],
                    gated[:, ki * T:(ki + 1) * T], start=(ki == 0), stop=(ki == NDT - 1)),
                    ins=[wb, gated], outs=[pp])
            p.emit("scalar", lambda e, mo=mo, pp=pp: e.activation(
                attn_fm[:, mo * T:(mo + 1) * T], pp[:, 0:T], AF.Copy),
                ins=[pp], outs=[attn_fm])
        pb = x_tm[0]
        for o in range(NKM):
            for tt in range(NT):
                pp = psTA if (o * NT + tt) % 2 == 0 else psTB
                p.emit("tensor", lambda e, o=o, tt=tt, pp=pp: e.transpose(
                    pp[:, 0:128], attn_fm[:, o * T + tt * 128:o * T + tt * 128 + 128],
                    ident16[:]), ins=[attn_fm, ident16], outs=[pp])
                p.emit("vector", lambda e, o=o, tt=tt, pp=pp, pb=pb: e.tensor_tensor(
                    pb[:, tt * DM + o * 128:tt * DM + o * 128 + 128],
                    pp[:, 0:128],
                    pb[:, tt * DM + o * 128:tt * DM + o * 128 + 128],
                    ALU.add), ins=[pp, pb], outs=[pb])
        p.dma("gpsimd", lambda e, c=c, pb=pb: e.dma_start(
            out_h.ap()[c * T:(c + 1) * T, :].rearrange("(i p) d -> p i d", p=128),
            pb[:].rearrange("p (i d) -> p i d", i=NT)),
            ins=[pb], outs=[out_h], sem="q_out", append=True)

    def b1_pieces(c):
        last = c == NCH - 1
        return [
            lambda: (stage_load_x(c), stage_ln(c, x_tm[0])),
            lambda: stage_quant(c),
            lambda: stage_ffn_mm(c),
            lambda: stage_ffn_partial(c),
            lambda: transpose_tm_to_fm(xn16, xn_fm, psTB),
            lambda: stage_inproj(c, range(0, 8)),
            lambda: stage_inproj(c, range(8, 16)),
            lambda: stage_inproj(c, range(16, 24)),
            lambda: stage_inproj(c, range(24, 32)),
            lambda: stage_conv(0, c % 2, range(0, 8)),
            lambda: stage_conv(0, c % 2, range(8, 16)),
            lambda: stage_xdbl(0, c % 2),
            lambda: stage_brep(c % 2, 0),
            lambda: stage_brep(c % 2, 1),
            lambda: stage_delta(0, c % 2, range(NDT)),
            lambda: stage_xspill_halo(c, last),
        ]

    def b2_pieces(c):
        g = c % 2
        return [
            lambda: p.dma("sync", [
                (lambda e, dt=dt, c=c: e.dma_start(
                    x_fm[:, dt * (T + 3):dt * (T + 3) + T],
                    x_sp.ap()[dt * 128:(dt + 1) * 128, c * T:(c + 1) * T]))
                for dt in range(NDT)], ins=[x_sp], outs=[x_fm], sem="q_x_fm"),
            None,
            None,
            lambda: stage_conv(1, g, range(0, 8)),
            lambda: stage_conv(1, g, range(8, 16)),
            lambda: ([p.emit("vector", lambda e, dt=dt: e.tensor_copy(
                x_fm[:, dt * (T + 3) + T:dt * (T + 3) + T + 3],
                x_fm[:, dt * (T + 3):dt * (T + 3) + 3]),
                ins=[x_fm], outs=[x_fm]) for dt in range(NDT)]
                if c > 0 else None),
            lambda: stage_xdbl(1, g),
            lambda: stage_brep(g, 0),
            lambda: stage_brep(g, 1),
            lambda: stage_delta(1, g, range(0, 8)),
            lambda: stage_delta(1, g, range(8, 16)),
            None, None, None, None, None, None,
        ]

    p.emit("vector", lambda e: e.memset(carry[:], 0.0), outs=[carry])
    for dt in range(NDT):
        p.emit("vector", lambda e, dt=dt: e.memset(
            x_fm[:, dt * (T + 3):dt * (T + 3) + 3], 0.0), outs=[x_fm])
    for f in b1_pieces(0):
        f()
    NU = 2 * NDT
    scan_prep(0, 0, 0, 0, rev=False, gu=0)
    scan_prep(0, 0, 0, 1, rev=False, gu=1)
    for c in range(NCH):
        g = c % 2
        nxt = b1_pieces(c + 1) if c + 1 < NCH else [None] * NDT
        for u in range(NU):
            gu = c * NU + u
            if u + 2 < NU:
                scan_prep(c, 0, g, u + 2, rev=False, gu=gu + 2)
            elif c + 1 < NCH:
                scan_prep(c + 1, 0, (c + 1) % 2, u + 2 - NU, rev=False, gu=gu + 2)
            scan_heavy(c, 0, g, u, rev=False, gu=gu)
            if u % 2 == 0 and nxt[u // 2] is not None:
                nxt[u // 2]()
            if u % 2 == 1:
                stage_y_fwd(c, u // 2)

    p.emit("vector", lambda e: e.memset(carry[:], 0.0), outs=[carry])
    for dt in range(NDT):
        p.emit("vector", lambda e, dt=dt: e.memset(
            x_fm[:, dt * (T + 3) + T:dt * (T + 3) + T + 3], 0.0), outs=[x_fm])
    for f in b2_pieces(NCH - 1):
        if f is not None:
            f()
    scan_prep(NCH - 1, 1, (NCH - 1) % 2, 0, rev=True, gu=0)
    scan_prep(NCH - 1, 1, (NCH - 1) % 2, 1, rev=True, gu=1)
    for c in range(NCH - 1, -1, -1):
        g = c % 2
        nxt = b2_pieces(c - 1) if c > 0 else [None] * NDT
        if c == NCH - 1:
            for dtp in range(3):
                yz_prefetch(c, dtp)
        stage_partial_prefetch(c)
        for u in range(NU):
            gu = (NCH - 1 - c) * NU + u
            if u + 2 < NU:
                scan_prep(c, 1, g, u + 2, rev=True, gu=gu + 2)
            elif c > 0:
                scan_prep(c - 1, 1, (c - 1) % 2, u + 2 - NU, rev=True, gu=gu + 2)
            scan_heavy(c, 1, g, u, rev=True, gu=gu)
            if u % 2 == 0:
                if nxt[u // 2] is not None:
                    nxt[u // 2]()
                if u // 2 + 3 < NDT:
                    yz_prefetch(c, u // 2 + 3)
                elif c > 0:
                    yz_prefetch(c - 1, u // 2 + 3 - NDT)
            if u % 2 == 1:
                stage_y_bwd(c, u // 2)
        stage_outproj_final(c)

    with nc.Block() as block:
        @block.sync
        def _(e):
            for f in p.prog["sync"]:
                f(e)

        @block.tensor
        def _(e):
            for f in p.prog["tensor"]:
                f(e)

        @block.vector
        def _(e):
            for f in p.prog["vector"]:
                f(e)

        @block.scalar
        def _(e):
            for f in p.prog["scalar"]:
                f(e)

        @block.gpsimd
        def _(e):
            for f in p.prog["gpsimd"]:
                f(e)

    st.close()
    return nc


def host_inputs(core_inp, b):
    f = np.ascontiguousarray
    w_in = core_inp["L0_attn_in_proj_w"]
    w_out = core_inp["L0_attn_out_proj_w"]
    wq = core_inp["L0_ffn_weight_q"]
    return {
        "hx": f(core_inp["L0_attn_hidden_states"][b]),
        "w_in_t": f(w_in.reshape(32, 128, 8, 128).transpose(0, 2, 3, 1)),
        "w_out_t": f(w_out.reshape(8, 128, 16, 128).transpose(0, 2, 3, 1)),
        "wq_t": f(wq.reshape(8, 128, 8, 128).transpose(2, 0, 3, 1)),
        "xproj_t": f(np.stack([
            core_inp["L0_attn_x_proj_w"].reshape(96, 16, 128).transpose(1, 2, 0),
            core_inp["L0_attn_x_proj_b_w"].reshape(96, 16, 128).transpose(1, 2, 0)])),
        "dtw_t": f(np.stack([
            core_inp["L0_attn_dt_proj_w"].reshape(16, 128, 64).transpose(0, 2, 1),
            core_inp["L0_attn_dt_proj_b_w"].reshape(16, 128, 64).transpose(0, 2, 1)])),
        "wconv": f(np.stack([
            core_inp["L0_attn_conv1d_w"].reshape(16, 128, 4),
            core_inp["L0_attn_conv1d_b_w"].reshape(16, 128, 4)])),
        "cbias": f(np.stack([
            core_inp["L0_attn_conv1d_bias"].reshape(16, 128),
            core_inp["L0_attn_conv1d_b_bias"].reshape(16, 128)])),
        "dtbias": f(np.stack([
            core_inp["L0_attn_dt_proj_bias"].reshape(16, 128),
            core_inp["L0_attn_dt_proj_b_bias"].reshape(16, 128)])),
        "alog": f(np.stack([
            core_inp["L0_attn_A_log"].reshape(16, 128, 16),
            core_inp["L0_attn_A_b_log"].reshape(16, 128, 16)])),
        "dvec": f(np.stack([
            core_inp["L0_attn_D"].reshape(16, 128),
            core_inp["L0_attn_D_b"].reshape(16, 128)])),
        "wsc": f(core_inp["L0_ffn_weight_scale"].reshape(8, 128)),
        "fbias": f(core_inp["L0_ffn_bias"].reshape(8, 128)),
    }


def kernel(**inputs):
    nc = build_program()
    in_maps = [host_inputs(inputs, b) for b in range(8)]
    res = run_bass_kernel_spmd(nc, in_maps, core_ids=list(range(8)))
    return np.stack([np.asarray(r["out"]) for r in res.results], axis=0)


# version 19
# speedup vs baseline: 1.0353x; 1.0059x over previous
import numpy as np
from contextlib import ExitStack

import concourse.bass as bass
import concourse.mybir as mybir
from concourse.bass_utils import run_bass_kernel_spmd

F32 = mybir.dt.float32
F16 = mybir.dt.float16
I8 = mybir.dt.int8
ALU = mybir.AluOpType
AF = mybir.ActivationFunctionType

L, DM, DI, NST = 8192, 1024, 2048, 16
NSH = NST // 2
T = 256
NCH = L // T
NT = T // 128
NDT = DI // 128
NKM = DM // 128
GRID = F16
ENGS = ("sync", "tensor", "vector", "scalar", "gpsimd")


class P:

    def __init__(self, nc, st):
        self.nc = nc
        self.st = st
        self.sems = {}
        self.count = {}
        self.prog = {e: [] for e in ENGS}
        self.floor = {e: {} for e in ENGS}
        self.bufs = {}
        for e in ENGS:
            self._sem(e)

    def _sem(self, name):
        if name not in self.sems:
            self.sems[name] = self.st.enter_context(self.nc.semaphore(name))
            self.count[name] = 0
        return name

    def _b(self, h):
        b = self.bufs.get(id(h))
        if b is None:
            b = {"w": {}, "r": []}
            self.bufs[id(h)] = b
        return b

    def _waits(self, eng, ins, outs, war=True):
        waits = {}
        fl = self.floor[eng]

        def need(tag):
            if tag is None:
                return
            s, v = tag
            if fl.get(s, 0) >= v:
                return
            if waits.get(s, 0) < v:
                waits[s] = v

        for h in ins:
            for t in self._b(h)["w"].items():
                need(t)
        for h in outs:
            b = self._b(h)
            if war:
                for t in b["w"].items():
                    need(t)
                for r in b["r"]:
                    need(r)
        wl = sorted(waits.items())
        for s, v in wl:
            fl[s] = v
        return wl

    def _push(self, eng, f, wl, sem_name, inc):
        sems = self.sems

        def replay(e, wl=wl, f=f, sem=sems[sem_name], inc=inc):
            for s, v in wl:
                e.wait_ge(sems[s], v)
            f(e).then_inc(sem, inc)

        self.prog[eng].append(replay)

    def _settle(self, ins, outs, tag, append=False):
        for h in outs:
            b = self._b(h)
            if append:
                b["w"][tag[0]] = tag[1]
            else:
                b["w"] = {tag[0]: tag[1]}
                b["r"] = []
        for h in ins:
            self._b(h)["r"].append(tag)

    def emit(self, eng, f, ins=(), outs=()):
        wl = self._waits(eng, ins, outs)
        self.count[eng] += 1
        self._push(eng, f, wl, eng, 1)
        self._settle(ins, outs, (eng, self.count[eng]))

    def dma(self, eng, fs, ins=(), outs=(), sem=None, append=False):
        if not isinstance(fs, (list, tuple)):
            fs = [fs]
        sem = self._sem(sem)
        wl = self._waits(eng, ins, outs, war=not append)
        first = True
        for f in fs:
            self.count[sem] += 16
            self._push(eng, f, wl if first else [], sem, 16)
            first = False
        self._settle(ins, outs, (sem, self.count[sem]), append=append)

    def wait(self, eng, sem, val):
        sems = self.sems
        self.prog[eng].append(lambda e, s=sem, v=val: e.wait_ge(sems[s], v))


def build_program():
    nc = bass.Bass("TRN2", target_bir_lowering=False)

    hx = nc.dram_tensor("hx", [L, DM], F32, kind="ExternalInput")
    w_in_t = nc.dram_tensor("w_in_t", [32, NKM, 128, 128], F32, kind="ExternalInput")
    w_out_t = nc.dram_tensor("w_out_t", [8, NDT, 128, 128], F32, kind="ExternalInput")
    wq_t = nc.dram_tensor("wq_t", [NKM, NKM, 128, 128], I8, kind="ExternalInput")
    xproj_t = nc.dram_tensor("xproj_t", [2, NDT, 128, 96], F32, kind="ExternalInput")
    dtw_t = nc.dram_tensor("dtw_t", [2, NDT, 64, 128], F32, kind="ExternalInput")
    wconv = nc.dram_tensor("wconv", [2, NDT, 128, 4], F32, kind="ExternalInput")
    cbias = nc.dram_tensor("cbias", [2, NDT, 128], F32, kind="ExternalInput")
    dtbias = nc.dram_tensor("dtbias", [2, NDT, 128], F32, kind="ExternalInput")
    alog = nc.dram_tensor("alog", [2, NDT, 128, NST], F32, kind="ExternalInput")
    dvec = nc.dram_tensor("dvec", [2, NDT, 128], F32, kind="ExternalInput")
    wsc = nc.dram_tensor("wsc", [NKM, 128], F32, kind="ExternalInput")
    fbias = nc.dram_tensor("fbias", [NKM, 128], F32, kind="ExternalInput")
    out_h = nc.dram_tensor("out", [L, DM], F32, kind="ExternalOutput")
    wc_in = nc.dram_tensor("wc_in", [32, NKM, 128, 128], F16)
    wc_out = nc.dram_tensor("wc_out", [8, NDT, 128, 128], F16)
    wc_q = nc.dram_tensor("wc_q", [NKM, NKM, 128, 128], F16)
    x_sp = nc.dram_tensor("x_sp", [DI, L], F16)
    z_sp = nc.dram_tensor("z_sp", [DI, L], F16)
    y_sp = nc.dram_tensor("y_sp", [DI, L], F16)
    part_sp = nc.dram_tensor("part_sp", [L, DM], F32)
    diag_dram = nc.dram_tensor("diag_dram", [2, NDT, 4, 128, 128], F16)

    st = ExitStack()
    sb = lambda name, shape, dt: st.enter_context(nc.sbuf_tensor(name, shape, dt))
    psa = lambda name, shape, dt: st.enter_context(nc.psum_tensor(name, shape, dt))

    ident16 = sb("ident16", [128, 128], F16)
    ident32 = sb("ident32", [128, 128], F32)
    ones32 = sb("ones32", [1, 128], F32)
    xprojT = sb("xprojT", [128, 2 * NDT * 96], F16)
    dtwT = sb("dtwT", [64, 2 * NDT * 128], F16)
    wcv16 = sb("wcv16", [128, 2 * NDT * 4], F16)
    diag_stg = [sb(f"diag{i}", [128, 4 * 128], F16) for i in range(2)]
    diag_bld = diag_stg[0]
    acol = sb("acol", [128, 2 * NDT * NST], F32)
    dcol = sb("dcol", [128, 2 * NDT], F32)
    cbcol = sb("cbcol", [128, 2 * NDT], F32)
    dtbcol = sb("dtbcol", [128, 2 * NDT], F32)
    wscol = sb("wscol", [128, NKM], F32)
    fbcol = sb("fbcol", [128, NKM], F32)
    wstg32 = sb("wstg32", [128, 3072], F32)
    wstg_i8 = sb("wstg_i8", [128, NKM * 128], I8)
    winstg = [sb(f"winstg{i}", [128, NKM * 128], F16) for i in range(2)]
    woutstg = [sb(f"woutstg{i}", [128, NDT * 128], F16) for i in range(2)]

    x_tm = [sb("x_tm0", [128, NT * DM], F32)]
    lncol = sb("lncol", [128, 24], F32)
    xn_tm = sb("xn_tm", [128, NT * DM], F32)
    xn16 = sb("xn16", [128, NT * DM], F16)
    xq16 = sb("xq16", [128, NT * DM], F16)
    xn_fm = sb("xn_fm", [128, NKM * T], F16)
    xq_fm = sb("xq_fm", [128, NKM * T], F16)
    xsrep = sb("xsrep", [128, T], F32)
    ffn_fm = sb("ffn_fm", [128, NKM * T], F16)
    attn_fm = ffn_fm
    x_fm = sb("x_fm", [128, NDT * (T + 3)], F16)
    dall = [sb(f"dall{i}", [128, NDT * T], F16) for i in range(2)]
    zstg = [sb(f"zstg{i}", [128, T], F16) for i in range(2)]
    xc_fm = [sb(f"xc_fm{i}", [128, NDT * T], F16) for i in range(2)]
    dtin = [sb(f"dtin{i}", [64, T], F16) for i in range(2)]
    bcf = [sb(f"bcf{i}", [128, T], F16) for i in range(2)]
    sel = sb("sel", [128, 32 * 128], F16)
    brep = [sb(f"brep{i}", [128, NST * T], F16) for i in range(2)]
    crep = [sb(f"crep{i}", [128, NST * T], F16) for i in range(2)]
    dsc1 = sb("dsc1", [128, T], F32)
    dsc2 = sb("dsc2", [128, T], F32)
    _uc = sb("uclip", [128, T], F16)
    uclip = [_uc, _uc]
    du = [sb(f"du_{i}", [128, T], F16) for i in range(2)]
    ud = [sb(f"ud{i}", [128, T], F32) for i in range(2)]
    dA = [sb(f"dA{i}", [128, NSH * T], GRID) for i in range(3)]
    beta = [sb(f"beta{i}", [128, NSH * T], GRID) for i in range(2)]
    fixb = [sb(f"fixb{i}", [128, NSH], F32) for i in range(2)]
    carry = sb("carry", [128, NDT * NST], F16)
    ystg = [sb(f"ystg{i}", [128, T], F16) for i in range(2)]
    yfl = [sb(f"yfl{i}", [128, T], F16) for i in range(4)]
    zfl = [sb(f"zfl{i}", [128, T], F16) for i in range(4)]
    _ys = sb("ysum", [128, T], F32)
    _gs = sb("gsil", [128, T], F16)
    ysum = [_ys, _ys]
    gsil = [_gs, _gs]
    sgbuf = [sb(f"sgbuf{i}", [128, T], F32) for i in range(2)]
    gated = sb("gated", [128, NDT * T], F16)

    psA = psa("psA", [128, 512], F32)
    psB = psa("psB", [128, 512], F32)
    psC = psa("psC", [128, 512], F32)
    psD = psa("psD", [128, 512], F32)
    psE = psa("psE", [128, 512], F32)
    psF = psa("psF", [128, 512], F32)
    psTA = psa("psTA", [128, 1024], F16)
    psTB = psa("psTB", [128, 1024], F16)

    p = P(nc, st)

    def seg3(t, n=NST):
        return t[:].rearrange("p (n t) -> p n t", n=n)

    def ld(dst_ap, src_ap, outs, slow=False, ins=()):
        p.dma("sync", lambda e, d=dst_ap, s=src_ap, sl=slow: e.dma_start(
            d, s, allow_slow_non_contiguous=sl), ins=ins, outs=outs,
            sem="q_" + outs[0].name)

    p.emit("gpsimd", lambda e: e.memset(ident16[:], 1.0), outs=[ident16])
    p.emit("gpsimd", lambda e: e.affine_select(
        ident16[:], ident16[:], [[-1, 128]], ALU.is_equal, 0.0, base=0,
        channel_multiplier=1), ins=[ident16], outs=[ident16])
    p.emit("gpsimd", lambda e: e.memset(ident32[:], 1.0), outs=[ident32])
    p.emit("gpsimd", lambda e: e.affine_select(
        ident32[:], ident32[:], [[-1, 128]], ALU.is_equal, 0.0, base=0,
        channel_multiplier=1), ins=[ident32], outs=[ident32])
    p.emit("gpsimd", lambda e: e.memset(ones32[:], 1.0), outs=[ones32])

    ld(wstg32[:, 0:2 * NDT * 4].rearrange("p (r dt k) -> p r dt k", r=2, dt=NDT),
       wconv.ap().rearrange("r dt p k -> p r dt k"), [wstg32])
    p.emit("vector", lambda e: e.tensor_copy(wcv16[:], wstg32[:, 0:2 * NDT * 4]),
           ins=[wstg32], outs=[wcv16])
    ld(cbcol[:].rearrange("p (r dt) -> p r dt", r=2),
       cbias.ap().rearrange("r dt p -> p r dt"), slow=True, outs=[cbcol])
    ld(dtbcol[:].rearrange("p (r dt) -> p r dt", r=2),
       dtbias.ap().rearrange("r dt p -> p r dt"), slow=True, outs=[dtbcol])
    ld(dcol[:].rearrange("p (r dt) -> p r dt", r=2),
       dvec.ap().rearrange("r dt p -> p r dt"), slow=True, outs=[dcol])
    ld(wscol[:], wsc.ap().rearrange("m p -> p m"), slow=True, outs=[wscol])
    ld(fbcol[:], fbias.ap().rearrange("m p -> p m"), slow=True, outs=[fbcol])
    ld(acol[:].rearrange("p (r dt n) -> p r dt n", r=2, dt=NDT),
       alog.ap().rearrange("r dt p n -> p r dt n"), [acol])
    p.emit("scalar", lambda e: e.activation(acol[:], acol[:], AF.Exp),
           ins=[acol], outs=[acol])
    p.emit("vector", lambda e: e.tensor_scalar(acol[:], acol[:], -1.0, None, ALU.mult),
           ins=[acol], outs=[acol])
    for r in range(2):
        ld(wstg32[:, 0:NDT * 96].rearrange("p (dt m) -> p dt m", dt=NDT),
           xproj_t.ap()[r].rearrange("dt k m -> k dt m"), [wstg32])
        p.emit("vector", lambda e, r=r: e.tensor_copy(
            xprojT[:, r * NDT * 96:(r + 1) * NDT * 96], wstg32[:, 0:NDT * 96]),
            ins=[wstg32], outs=[xprojT])
        ld(wstg32[0:64, 0:NDT * 128].rearrange("p (dt m) -> p dt m", dt=NDT),
           dtw_t.ap()[r].rearrange("dt k m -> k dt m"), [wstg32])
        p.emit("vector", lambda e, r=r: e.tensor_copy(
            dtwT[:, r * NDT * 128:(r + 1) * NDT * 128],
            wstg32[0:64, 0:NDT * 128]), ins=[wstg32], outs=[dtwT])

    for mt in range(32):
        ld(wstg32[:, 0:NKM * 128].rearrange("p (kt m) -> p kt m", kt=NKM),
           w_in_t.ap()[mt].rearrange("kt k m -> k kt m"), [wstg32])
        p.emit("vector", lambda e: e.tensor_copy(
            winstg[0][:], wstg32[:, 0:NKM * 128]), ins=[wstg32], outs=[winstg[0]])
        p.dma("gpsimd", lambda e, mt=mt: e.dma_start(
            wc_in.ap()[mt].rearrange("kt k m -> k kt m"),
            winstg[0][:].rearrange("p (kt m) -> p kt m", kt=NKM)),
            ins=[winstg[0]], outs=[wc_in], sem="q_wc_in", append=True)
    for mo in range(8):
        for half in range(2):
            ld(wstg32[:, 0:NKM * 128].rearrange("p (a m) -> p a m", a=NKM),
               w_out_t.ap()[mo, half * NKM:(half + 1) * NKM].rearrange(
                   "kt k m -> k kt m"), [wstg32])
            p.emit("vector", lambda e: e.tensor_copy(
                winstg[0][:], wstg32[:, 0:NKM * 128]), ins=[wstg32], outs=[winstg[0]])
            p.dma("gpsimd", lambda e, mo=mo, half=half: e.dma_start(
                wc_out.ap()[mo, half * NKM:(half + 1) * NKM].rearrange(
                    "kt k m -> k kt m"),
                winstg[0][:].rearrange("p (kt m) -> p kt m", kt=NKM)),
                ins=[winstg[0]], outs=[wc_out], sem="q_wc_out", append=True)
    for ki in range(NKM):
        ld(wstg_i8[:].rearrange("p (mo m) -> p mo m", mo=NKM),
           wq_t.ap()[ki].rearrange("mo k m -> k mo m"), [wstg_i8])
        p.emit("vector", lambda e: e.tensor_copy(
            winstg[0][:], wstg_i8[:]),
            ins=[wstg_i8], outs=[winstg[0]])
        p.dma("gpsimd", lambda e, ki=ki: e.dma_start(
            wc_q.ap()[ki].rearrange("mo k m -> k mo m"),
            winstg[0][:].rearrange("p (mo m) -> p mo m", mo=NKM)),
            ins=[winstg[0]], outs=[wc_q], sem="q_wc_q", append=True)

    p.emit("gpsimd", lambda e: e.memset(sel[:], 1.0), outs=[sel])
    for n in range(32):
        p.emit("gpsimd", lambda e, n=n: e.affine_select(
            sel[64:96, n * 128:(n + 1) * 128], sel[64:96, n * 128:(n + 1) * 128],
            [[0, 128]], ALU.is_equal, 0.0, base=-n, channel_multiplier=1),
            ins=[sel], outs=[sel])

    def build_diag(r):
        for dt in range(NDT):
            for k in range(4):
                src_k = k if r == 0 else 3 - k
                wslice = wcv16[:, (r * NDT + dt) * 4 + src_k:
                               (r * NDT + dt) * 4 + src_k + 1]
                dst = diag_bld[:, k * 128:(k + 1) * 128]
                p.emit("gpsimd", lambda e, d=dst, w=wslice: e.affine_select(
                    d, w.broadcast_to((128, 128)), [[-1, 128]], ALU.is_equal, 0.0,
                    base=0, channel_multiplier=1), ins=[wcv16], outs=[diag_bld])
            p.dma("gpsimd", lambda e, r=r, dt=dt: e.dma_start(
                diag_dram.ap()[r, dt].rearrange("k p m -> p k m"),
                diag_bld[:].rearrange("p (k m) -> p k m", k=4)),
                ins=[diag_bld], outs=[diag_dram], sem="q_diag", append=True)

    build_diag(0)
    build_diag(1)

    build_diag(1)

    def stage_load_x(c):
        xb = x_tm[0]
        ld(xb[:].rearrange("p (i d) -> p i d", i=NT),
           hx.ap()[c * T:(c + 1) * T, :].rearrange("(i p) d -> p i d", p=128), [xb])
        return xb

    def stage_ln(c, xb):
        scr = wstg32[:, 2048:3072]
        for tt in range(NT):
            xs = xb[:, tt * DM:(tt + 1) * DM]
            p.emit("scalar", lambda e, xs=xs, scr=scr: e.activation(
                scr, xs, AF.Copy, accum_out=lncol[:, 0:1]),
                ins=[xb], outs=[wstg32, lncol])
            p.emit("scalar", lambda e, xs=xs, scr=scr: e.activation(
                scr, xs, AF.Square, accum_out=lncol[:, 1:2]),
                ins=[xb], outs=[wstg32, lncol])
            p.emit("vector", lambda e: e.tensor_scalar(
                lncol[:, 2:3], lncol[:, 0:1], 1.0 / DM, None, ALU.mult),
                ins=[lncol], outs=[lncol])
            p.emit("vector", lambda e: e.tensor_scalar(
                lncol[:, 3:4], lncol[:, 1:2], 1.0 / DM, None, ALU.mult),
                ins=[lncol], outs=[lncol])
            p.emit("vector", lambda e: e.tensor_tensor(
                lncol[:, 4:5], lncol[:, 2:3], lncol[:, 2:3], ALU.mult),
                ins=[lncol], outs=[lncol])
            p.emit("vector", lambda e: e.tensor_tensor(
                lncol[:, 5:6], lncol[:, 3:4], lncol[:, 4:5], ALU.subtract),
                ins=[lncol], outs=[lncol])
            p.emit("vector", lambda e: e.tensor_scalar(
                lncol[:, 5:6], lncol[:, 5:6], 1e-5, None, ALU.add),
                ins=[lncol], outs=[lncol])
            p.emit("scalar", lambda e: e.activation(
                lncol[:, 6:7], lncol[:, 5:6], AF.Ln), ins=[lncol], outs=[lncol])
            p.emit("scalar", lambda e: e.activation(
                lncol[:, 7:8], lncol[:, 6:7], AF.Exp, scale=-0.5),
                ins=[lncol], outs=[lncol])
            p.emit("vector", lambda e: e.tensor_tensor(
                lncol[:, 10:11], lncol[:, 2:3], lncol[:, 7:8], ALU.mult),
                ins=[lncol], outs=[lncol])
            p.emit("vector", lambda e: e.tensor_scalar(
                lncol[:, 10:11], lncol[:, 10:11], -1.0, None, ALU.mult),
                ins=[lncol], outs=[lncol])
            p.emit("vector", lambda e, xs=xs, tt=tt: e.tensor_scalar(
                xn_tm[:, tt * DM:(tt + 1) * DM], xs, lncol[:, 7:8], lncol[:, 10:11],
                ALU.mult, ALU.add), ins=[xb, lncol], outs=[xn_tm])
        p.emit("vector", lambda e: e.tensor_copy(xn16[:], xn_tm[:]),
               ins=[xn_tm], outs=[xn16])

    def stage_quant(c):
        for tt in range(NT):
            xs = xn_tm[:, tt * DM:(tt + 1) * DM]
            q1s = wstg32[:, 0:DM]
            q2s = wstg32[:, DM:2 * DM]
            p.emit("vector", lambda e, xs=xs: e.tensor_reduce(
                lncol[:, 12:13], xs, mybir.AxisListType.X, ALU.max,
                apply_absolute_value=True), ins=[xn_tm], outs=[lncol])
            p.emit("vector", lambda e: e.tensor_scalar(
                lncol[:, 13:14], lncol[:, 12:13], 1e-6, 1.0 / 127.0, ALU.max, ALU.mult),
                ins=[lncol], outs=[lncol])
            p.emit("vector", lambda e: e.reciprocal(lncol[:, 14:15], lncol[:, 13:14]),
                   ins=[lncol], outs=[lncol])
            p.emit("vector", lambda e, tt=tt: e.tensor_copy(
                lncol[:, 16 + tt:17 + tt], lncol[:, 13:14]), ins=[lncol], outs=[lncol])
            p.emit("vector", lambda e, xs=xs, q1s=q1s: e.tensor_scalar(
                q1s, xs, lncol[:, 14:15], None, ALU.mult),
                ins=[xn_tm, lncol], outs=[wstg32])
            p.emit("vector", lambda e, q1s=q1s: e.tensor_scalar(
                q1s, q1s, -128.0, 127.0, ALU.max, ALU.min), ins=[wstg32], outs=[wstg32])
            p.emit("vector", lambda e, q1s=q1s: e.tensor_scalar(
                q1s, q1s, 128.5, None, ALU.add), ins=[wstg32], outs=[wstg32])
            p.emit("vector", lambda e, q1s=q1s, q2s=q2s: e.tensor_copy(
                q2s, q1s), ins=[wstg32], outs=[wstg32])
            q3s = wstg32[:, 2 * DM:3 * DM]
            for bit in (128.0, 64.0, 32.0, 16.0, 8.0, 4.0, 2.0, 1.0):
                p.emit("vector", lambda e, q2s=q2s, q3s=q3s, b=bit: e.tensor_scalar(
                    q3s, q2s, b, -b, ALU.is_ge, ALU.mult), ins=[wstg32], outs=[wstg32])
                p.emit("vector", lambda e, q2s=q2s, q3s=q3s: e.tensor_tensor(
                    q2s, q2s, q3s, ALU.add), ins=[wstg32], outs=[wstg32])
            p.emit("vector", lambda e, q1s=q1s, q2s=q2s: e.scalar_tensor_tensor(
                q1s, q2s, -1.0, q1s, ALU.mult, ALU.add), ins=[wstg32], outs=[wstg32])
            p.emit("vector", lambda e, q1s=q1s, tt=tt: e.tensor_scalar(
                xq16[:, tt * DM:(tt + 1) * DM], q1s, -128.0, None, ALU.add),
                ins=[wstg32], outs=[xq16])
        p.dma("sync", [
            (lambda e, tt=tt: e.dma_start(
                wstg32[0:1, 2048 + tt * 128:2048 + (tt + 1) * 128], lncol[:, 16 + tt:17 + tt],
                allow_slow_non_contiguous=True))
            for tt in range(NT)], ins=[lncol], outs=[wstg32], sem="q_xs_row")
        p.emit("tensor", lambda e: e.matmul(
            psF[:, 0:T], ones32[:], wstg32[0:1, 2048:2048 + T], start=True, stop=True),
            ins=[ones32, wstg32], outs=[psF])
        p.emit("vector", lambda e: e.tensor_copy(xsrep[:], psF[:, 0:T]),
               ins=[psF], outs=[xsrep])

    def transpose_tm_to_fm(src16, dst16, pst):
        for kt in range(NKM):
            for tt in range(NT):
                slot = ((kt * NT + tt) % 8) * 128
                p.emit("tensor", lambda e, kt=kt, tt=tt, slot=slot, s=src16, pp=pst:
                       e.transpose(pp[:, slot:slot + 128],
                                   s[:, tt * DM + kt * 128:tt * DM + kt * 128 + 128],
                                   ident16[:]), ins=[src16, ident16], outs=[pst])
                p.emit("vector", lambda e, kt=kt, tt=tt, slot=slot, d=dst16, pp=pst:
                       e.tensor_copy(d[:, kt * T + tt * 128:kt * T + tt * 128 + 128],
                                     pp[:, slot:slot + 128]), ins=[pst], outs=[dst16])

    def stage_ffn_mm(c):
        transpose_tm_to_fm(xq16, xq_fm, psTA)
        for mo in range(NKM):
            wb = winstg[mo % 2]
            ld(wb[:].rearrange("p (ki m) -> p ki m", ki=NKM),
               wc_q.ap()[:, mo].rearrange("ki k m -> k ki m"), [wb], ins=[wc_q])
            pp = psA if mo % 2 == 0 else psB
            for ki in range(NKM):
                p.emit("tensor", lambda e, ki=ki, wb=wb, pp=pp: e.matmul(
                    pp[:, 0:T], wb[:, ki * 128:(ki + 1) * 128],
                    xq_fm[:, ki * T:(ki + 1) * T], start=(ki == 0), stop=(ki == NKM - 1)),
                    ins=[wb, xq_fm], outs=[pp])
            p.emit("vector", lambda e, mo=mo, pp=pp: e.tensor_tensor(
                ffn_fm[:, mo * T:(mo + 1) * T], pp[:, 0:T], xsrep[:], ALU.mult),
                ins=[pp, xsrep], outs=[ffn_fm])
            p.emit("scalar", lambda e, mo=mo: e.activation(
                ffn_fm[:, mo * T:(mo + 1) * T], ffn_fm[:, mo * T:(mo + 1) * T],
                AF.Identity, bias=fbcol[:, mo:mo + 1], scale=wscol[:, mo:mo + 1]),
                ins=[ffn_fm, wscol, fbcol], outs=[ffn_fm])

    def stage_ffn_partial(c):
        xb = x_tm[0]
        for o in range(NKM):
            for tt in range(NT):
                pp = psTA if (o * NT + tt) % 2 == 0 else psTB
                p.emit("tensor", lambda e, o=o, tt=tt, pp=pp: e.transpose(
                    pp[:, 0:128], ffn_fm[:, o * T + tt * 128:o * T + tt * 128 + 128],
                    ident16[:]), ins=[ffn_fm, ident16], outs=[pp])
                p.emit("vector", lambda e, o=o, tt=tt, pp=pp, xb=xb: e.tensor_tensor(
                    xb[:, tt * DM + o * 128:tt * DM + o * 128 + 128],
                    pp[:, 0:128], xb[:, tt * DM + o * 128:tt * DM + o * 128 + 128],
                    ALU.add), ins=[pp, xb], outs=[xb])
        p.dma("gpsimd", lambda e, c=c, xb=xb: e.dma_start(
            part_sp.ap()[c * T:(c + 1) * T, :].rearrange("(i p) d -> p i d", p=128),
            xb[:].rearrange("p (i d) -> p i d", i=NT)),
            ins=[xb], outs=[part_sp], sem="q_part_sp", append=True)

    def stage_inproj(c, mts):
        for mt in mts:
            wb = winstg[mt % 2]
            ld(wb[:].rearrange("p (kt m) -> p kt m", kt=NKM),
               wc_in.ap()[mt].rearrange("kt k m -> k kt m"), [wb], ins=[wc_in])
            pp = psA if mt % 2 == 0 else psB
            for kt in range(NKM):
                p.emit("tensor", lambda e, kt=kt, wb=wb, pp=pp: e.matmul(
                    pp[:, 0:T], wb[:, kt * 128:(kt + 1) * 128],
                    xn_fm[:, kt * T:(kt + 1) * T], start=(kt == 0), stop=(kt == NKM - 1)),
                    ins=[wb, xn_fm], outs=[pp])
            if mt < NDT:
                dst = x_fm[:, mt * (T + 3) + 3: mt * (T + 3) + 3 + T]
                p.emit("scalar", lambda e, dst=dst, pp=pp: e.activation(
                    dst, pp[:, 0:T], AF.Copy), ins=[pp], outs=[x_fm])
            else:
                zb = zstg[mt % 2]
                p.emit("scalar", lambda e, zb=zb, pp=pp: e.activation(
                    zb[:], pp[:, 0:T], AF.Copy), ins=[pp], outs=[zb])
                p.dma("gpsimd", lambda e, mt=mt, c=c, zb=zb: e.dma_start(
                    z_sp.ap()[(mt - NDT) * 128:(mt - NDT + 1) * 128, c * T:(c + 1) * T],
                    zb[:]), ins=[zb], outs=[z_sp], sem=f"q_st_z{mt % 2}", append=True)

    def stage_xspill_halo(c, last):
        p.dma("gpsimd", [
            (lambda e, mt=mt, c=c: e.dma_start(
                x_sp.ap()[mt * 128:(mt + 1) * 128, c * T:(c + 1) * T],
                x_fm[:, mt * (T + 3) + 3: mt * (T + 3) + 3 + T]))
            for mt in range(NDT)], ins=[x_fm], outs=[x_sp], sem="q_st_x")
        if not last:
            for dt in range(NDT):
                p.emit("vector", lambda e, dt=dt: e.tensor_copy(
                    x_fm[:, dt * (T + 3):dt * (T + 3) + 3],
                    x_fm[:, dt * (T + 3) + T:dt * (T + 3) + T + 3]),
                    ins=[x_fm], outs=[x_fm])

    def stage_conv(r, g, dts):
        dts = list(dts)
        ld(diag_stg[dts[0] % 2][:].rearrange("p (k m) -> p k m", k=4),
           diag_dram.ap()[r, dts[0]].rearrange("k p m -> p k m"),
           [diag_stg[dts[0] % 2]], ins=[diag_dram])
        for i, dt in enumerate(dts):
            db = diag_stg[dt % 2]
            if i + 1 < len(dts):
                nxt_dt = dts[i + 1]
                ld(diag_stg[nxt_dt % 2][:].rearrange("p (k m) -> p k m", k=4),
                   diag_dram.ap()[r, nxt_dt].rearrange("k p m -> p k m"),
                   [diag_stg[nxt_dt % 2]], ins=[diag_dram])
            pp = psC if dt % 2 == 0 else psD
            for k in range(4):
                p.emit("tensor", lambda e, dt=dt, k=k, pp=pp, db=db: e.matmul(
                    pp[:, 0:T], db[:, k * 128:(k + 1) * 128],
                    x_fm[:, dt * (T + 3) + k: dt * (T + 3) + k + T],
                    start=(k == 0), stop=(k == 3)), ins=[db, x_fm], outs=[pp])
            p.emit("scalar", lambda e, dt=dt, r=r, pp=pp, g=g: e.activation(
                xc_fm[g][:, dt * T:(dt + 1) * T], pp[:, 0:T], AF.Identity,
                bias=cbcol[:, r * NDT + dt:r * NDT + dt + 1]),
                ins=[pp, cbcol], outs=[xc_fm[g]])

    def stage_xdbl(r, g):
        for kt in range(NDT):
            p.emit("tensor", lambda e, kt=kt, r=r, g=g: e.matmul(
                psC[0:96, 0:T], xprojT[:, (r * NDT + kt) * 96:(r * NDT + kt + 1) * 96],
                xc_fm[g][:, kt * T:(kt + 1) * T], start=(kt == 0),
                stop=(kt == NDT - 1)), ins=[xprojT, xc_fm[g]], outs=[psC])
        p.emit("scalar", lambda e, g=g: e.activation(
            dtin[g][:], psC[0:64, 0:T], AF.Copy), ins=[psC], outs=[dtin[g]])
        p.emit("scalar", lambda e, g=g: e.activation(
            bcf[g][64:96, :], psC[64:96, 0:T], AF.Copy), ins=[psC], outs=[bcf[g]])

    def stage_brep(g, which):
        rep = brep[g] if which == 0 else crep[g]
        base = 0 if which == 0 else 16
        for n in range(NST):
            pp = psE if n % 2 == 0 else psF
            p.emit("tensor", lambda e, n=n, pp=pp, g=g: e.matmul(
                pp[:, 0:T], sel[64:96, (base + n) * 128:(base + n + 1) * 128],
                bcf[g][64:96, :], start=True, stop=True),
                ins=[sel, bcf[g]], outs=[pp])
            if n % 2 == 0:
                p.emit("vector", lambda e, n=n, pp=pp, rep=rep: e.tensor_copy(
                    rep[:, n * T:(n + 1) * T], pp[:, 0:T]), ins=[pp], outs=[rep])
            else:
                p.emit("scalar", lambda e, n=n, pp=pp, rep=rep: e.activation(
                    rep[:, n * T:(n + 1) * T], pp[:, 0:T], AF.Copy),
                    ins=[pp], outs=[rep])

    def stage_delta(r, g, dts):
        for dt in dts:
            idx = r * NDT + dt
            pp = psC if dt % 2 == 0 else psD
            p.emit("tensor", lambda e, idx=idx, pp=pp, g=g: e.matmul(
                pp[:, 0:T], dtwT[:, idx * 128:(idx + 1) * 128], dtin[g][:],
                start=True, stop=True), ins=[dtwT, dtin[g]], outs=[pp])
            p.emit("vector", lambda e, idx=idx, pp=pp: e.tensor_scalar(
                dsc1[:], pp[:, 0:T], dtbcol[:, idx:idx + 1], 1e-5, ALU.add, ALU.max),
                ins=[pp, dtbcol], outs=[dsc1])
            p.emit("vector", lambda e, idx=idx: e.tensor_scalar(
                dsc2[:], dsc1[:], 1.0, dtbcol[:, idx:idx + 1], ALU.min, ALU.add),
                ins=[dsc1, dtbcol], outs=[dsc2])
            p.emit("scalar", lambda e: e.activation(dsc1[:], dsc2[:], AF.Exp),
                   ins=[dsc2], outs=[dsc1])
            p.emit("scalar", lambda e, dt=dt, g=g: e.activation(
                dall[g][:, dt * T:(dt + 1) * T], dsc1[:], AF.Ln, bias=1.0),
                ins=[dsc1], outs=[dall[g]])

    def scan_prep(c, r, g, u, rev, gu):
        dt, h = u // 2, u % 2
        idx = r * NDT + dt
        dA_ = dA[gu % 3]
        dslice = dall[g][:, dt * T:(dt + 1) * T]
        if h == 0:
            du_, ud_, uclip_ = du[dt % 2], ud[dt % 2], uclip[dt % 2]
            p.emit("vector", lambda e, dt=dt, uclip_=uclip_, g=g: e.tensor_scalar(
                uclip_[:], xc_fm[g][:, dt * T:(dt + 1) * T], 10.0, -10.0,
                ALU.min, ALU.max), ins=[xc_fm[g]], outs=[uclip_])
            p.emit("vector", lambda e, du_=du_, ds=dslice, uclip_=uclip_, g=g:
                   e.tensor_tensor(du_[:], ds, uclip_[:], ALU.mult),
                   ins=[dall[g], uclip_], outs=[du_])
            p.emit("vector", lambda e, idx=idx, ud_=ud_, uclip_=uclip_:
                   e.tensor_scalar(ud_[:], uclip_[:], dcol[:, idx:idx + 1], None,
                                   ALU.mult), ins=[uclip_, dcol], outs=[ud_])
        din = dslice[:, ::-1] if rev else dslice
        for n in range(NSH):
            na = h * NSH + n
            p.emit("scalar", lambda e, n=n, na=na, idx=idx, din=din, dA_=dA_, g=g:
                   e.activation(dA_[:, n * T:(n + 1) * T], din, AF.Exp,
                                scale=acol[:, idx * NST + na:idx * NST + na + 1]),
                   ins=[dall[g], acol], outs=[dA_])

    def seg3h(t):
        return t[:].rearrange("p (n t) -> p n t", n=NSH)

    def scan_heavy(c, r, g, u, rev, gu):
        dt, h = u // 2, u % 2
        par = u % 2
        dA_, beta_ = dA[gu % 3], beta[par]
        du16_ = du[dt % 2]
        fx = fixb[par]
        bslice = brep[g][:, h * NSH * T:(h + 1) * NSH * T]
        cslice = crep[g][:, h * NSH * T:(h + 1) * NSH * T]
        b3 = bslice.rearrange("p (n t) -> p n t", n=NSH)
        c3 = cslice.rearrange("p (n t) -> p n t", n=NSH)
        if rev:
            du_b = du16_[:, ::-1].unsqueeze(1).broadcast_to((128, NSH, T))
            br_b = b3[:, :, ::-1]
        else:
            du_b = du16_[:].unsqueeze(1).broadcast_to((128, NSH, T))
            br_b = b3
        p.emit("vector", lambda e, du_b=du_b, br_b=br_b, beta_=beta_: e.tensor_tensor(
            seg3h(beta_), du_b, br_b, ALU.mult), ins=[du16_, brep[g]], outs=[beta_])
        cs = carry[:, dt * NST + h * NSH: dt * NST + (h + 1) * NSH]
        p.emit("vector", lambda e, cs=cs, dA_=dA_, fx=fx: e.tensor_tensor(
            fx[:].rearrange("p (n o) -> p n o", o=1),
            seg3h(dA_)[:, :, 0:1], cs.rearrange("p (n o) -> p n o", o=1), ALU.mult),
            ins=[dA_, carry], outs=[fx])
        p.emit("vector", lambda e, beta_=beta_, fx=fx: e.tensor_tensor(
            seg3h(beta_)[:, :, 0:1], seg3h(beta_)[:, :, 0:1],
            fx[:].rearrange("p (n o) -> p n o", o=1), ALU.add),
            ins=[beta_, fx], outs=[beta_])
        p.emit("vector", lambda e, dA_=dA_: e.memset(seg3h(dA_)[:, :, 0:1], 0.0),
               ins=[dA_], outs=[dA_])
        p.emit("vector", lambda e, dA_=dA_, beta_=beta_: e.tensor_tensor_scan(
            dA_[:], dA_[:], beta_[:], 0.0, ALU.mult, ALU.add),
            ins=[dA_, beta_], outs=[dA_])
        p.emit("vector", lambda e, cs=cs, dA_=dA_: e.tensor_copy(
            cs.rearrange("p (n o) -> p n o", o=1), seg3h(dA_)[:, :, T - 1:T]),
            ins=[dA_], outs=[carry])
        cr_b = c3[:, :, ::-1] if rev else c3
        p.emit("vector", lambda e, cr_b=cr_b, dA_=dA_, beta_=beta_: e.tensor_tensor(
            seg3h(beta_), seg3h(dA_), cr_b, ALU.mult), ins=[dA_, crep[g]], outs=[beta_])
        w = NSH * T
        while w > T:
            w //= 2
            p.emit("vector", lambda e, w=w, beta_=beta_: e.tensor_tensor(
                beta_[:, 0:w], beta_[:, 0:w], beta_[:, w:2 * w], ALU.add),
                ins=[beta_], outs=[beta_])

    def stage_y_fwd(c, dt):
        yb = ystg[dt % 2]
        p.emit("vector", lambda e, yb=yb: e.tensor_tensor(
            yb[:], beta[0][:, 0:T], beta[1][:, 0:T], ALU.add),
            ins=[beta[0], beta[1]], outs=[yb])
        p.emit("vector", lambda e, yb=yb, dt=dt: e.tensor_tensor(
            yb[:], yb[:], ud[dt % 2][:], ALU.add), ins=[yb, ud[dt % 2]], outs=[yb])
        p.dma("gpsimd", lambda e, dt=dt, c=c, yb=yb: e.dma_start(
            y_sp.ap()[dt * 128:(dt + 1) * 128, c * T:(c + 1) * T], yb[:]),
            ins=[yb], outs=[y_sp], sem=f"q_st_y{dt % 2}", append=True)

    def yz_prefetch(c, dt):
        yfb, zb = yfl[dt % 4], zfl[dt % 4]
        ld(yfb[:], y_sp.ap()[dt * 128:(dt + 1) * 128, c * T:(c + 1) * T], [yfb],
           ins=[y_sp])
        ld(zb[:], z_sp.ap()[dt * 128:(dt + 1) * 128, c * T:(c + 1) * T], [zb],
           ins=[z_sp])

    def stage_sig(c, dt):
        par = dt % 2
        zb, sg = zfl[dt % 4], sgbuf[par]
        p.emit("scalar", lambda e, zb=zb, sg=sg: e.activation(
            sg[:], zb[:], AF.Exp, scale=-1.0), ins=[zb], outs=[sg])
        p.emit("scalar", lambda e, sg=sg: e.activation(
            sg[:], sg[:], AF.Ln, bias=1.0), ins=[sg], outs=[sg])
        p.emit("scalar", lambda e, sg=sg: e.activation(
            sg[:], sg[:], AF.Exp, scale=-1.0), ins=[sg], outs=[sg])

    def stage_y_bwd(c, dt):
        par = dt % 2
        yfb, zb = yfl[dt % 4], zfl[dt % 4]
        ys, sg, gs = ysum[par], sgbuf[par], gsil[par]
        p.emit("vector", lambda e, ys=ys: e.tensor_tensor(
            ys[:], beta[0][:, 0:T][:, ::-1], beta[1][:, 0:T][:, ::-1], ALU.add),
            ins=[beta[0], beta[1]], outs=[ys])
        p.emit("vector", lambda e, ys=ys, dt=dt: e.tensor_tensor(
            ys[:], ys[:], ud[dt % 2][:], ALU.add), ins=[ys, ud[dt % 2]], outs=[ys])
        p.emit("vector", lambda e, yfb=yfb, ys=ys: e.tensor_tensor(
            ys[:], ys[:], yfb[:], ALU.add), ins=[ys, yfb], outs=[ys])
        p.emit("vector", lambda e, zb=zb, sg=sg, gs=gs: e.tensor_tensor(
            gs[:], zb[:], sg[:], ALU.mult), ins=[zb, sg], outs=[gs])
        p.emit("vector", lambda e, dt=dt, ys=ys, gs=gs: e.tensor_tensor(
            gated[:, dt * T:(dt + 1) * T], ys[:], gs[:], ALU.mult),
            ins=[ys, gs], outs=[gated])

    def stage_partial_prefetch(c):
        pb = x_tm[0]
        ld(pb[:].rearrange("p (i d) -> p i d", i=NT),
           part_sp.ap()[c * T:(c + 1) * T, :].rearrange("(i p) d -> p i d", p=128),
           [pb], ins=[part_sp])

    def stage_outproj_final(c):
        for mo in range(NKM):
            wb = woutstg[mo % 2]
            ld(wb[:].rearrange("p (kt m) -> p kt m", kt=NDT),
               wc_out.ap()[mo].rearrange("kt k m -> k kt m"), [wb], ins=[wc_out])
            pp = psA if mo % 2 == 0 else psB
            for ki in range(NDT):
                p.emit("tensor", lambda e, ki=ki, wb=wb, pp=pp: e.matmul(
                    pp[:, 0:T], wb[:, ki * 128:(ki + 1) * 128],
                    gated[:, ki * T:(ki + 1) * T], start=(ki == 0), stop=(ki == NDT - 1)),
                    ins=[wb, gated], outs=[pp])
            p.emit("scalar", lambda e, mo=mo, pp=pp: e.activation(
                attn_fm[:, mo * T:(mo + 1) * T], pp[:, 0:T], AF.Copy),
                ins=[pp], outs=[attn_fm])
        pb = x_tm[0]
        for o in range(NKM):
            for tt in range(NT):
                pp = psTA if (o * NT + tt) % 2 == 0 else psTB
                p.emit("tensor", lambda e, o=o, tt=tt, pp=pp: e.transpose(
                    pp[:, 0:128], attn_fm[:, o * T + tt * 128:o * T + tt * 128 + 128],
                    ident16[:]), ins=[attn_fm, ident16], outs=[pp])
                p.emit("vector", lambda e, o=o, tt=tt, pp=pp, pb=pb: e.tensor_tensor(
                    pb[:, tt * DM + o * 128:tt * DM + o * 128 + 128],
                    pp[:, 0:128],
                    pb[:, tt * DM + o * 128:tt * DM + o * 128 + 128],
                    ALU.add), ins=[pp, pb], outs=[pb])
        p.dma("gpsimd", lambda e, c=c, pb=pb: e.dma_start(
            out_h.ap()[c * T:(c + 1) * T, :].rearrange("(i p) d -> p i d", p=128),
            pb[:].rearrange("p (i d) -> p i d", i=NT)),
            ins=[pb], outs=[out_h], sem="q_out", append=True)

    def b1_pieces(c):
        last = c == NCH - 1
        return [
            lambda: (stage_load_x(c), stage_ln(c, x_tm[0])),
            lambda: stage_quant(c),
            lambda: stage_ffn_mm(c),
            lambda: stage_ffn_partial(c),
            lambda: transpose_tm_to_fm(xn16, xn_fm, psTB),
            lambda: stage_inproj(c, range(0, 8)),
            lambda: stage_inproj(c, range(8, 16)),
            lambda: stage_inproj(c, range(16, 24)),
            lambda: stage_inproj(c, range(24, 32)),
            lambda: stage_conv(0, c % 2, range(0, 8)),
            lambda: stage_conv(0, c % 2, range(8, 16)),
            lambda: stage_xdbl(0, c % 2),
            lambda: stage_brep(c % 2, 0),
            lambda: stage_brep(c % 2, 1),
            lambda: stage_delta(0, c % 2, range(NDT)),
            lambda: stage_xspill_halo(c, last),
        ]

    def b2_pieces(c):
        g = c % 2
        return [
            lambda: p.dma("sync", [
                (lambda e, dt=dt, c=c: e.dma_start(
                    x_fm[:, dt * (T + 3):dt * (T + 3) + T],
                    x_sp.ap()[dt * 128:(dt + 1) * 128, c * T:(c + 1) * T]))
                for dt in range(NDT)], ins=[x_sp], outs=[x_fm], sem="q_x_fm"),
            None,
            None,
            lambda: stage_conv(1, g, range(0, 8)),
            lambda: stage_conv(1, g, range(8, 16)),
            lambda: ([p.emit("vector", lambda e, dt=dt: e.tensor_copy(
                x_fm[:, dt * (T + 3) + T:dt * (T + 3) + T + 3],
                x_fm[:, dt * (T + 3):dt * (T + 3) + 3]),
                ins=[x_fm], outs=[x_fm]) for dt in range(NDT)]
                if c > 0 else None),
            lambda: stage_xdbl(1, g),
            lambda: stage_brep(g, 0),
            lambda: stage_brep(g, 1),
            lambda: stage_delta(1, g, range(0, 8)),
            lambda: stage_delta(1, g, range(8, 16)),
            None, None, None, None, None, None,
        ]

    p.emit("vector", lambda e: e.memset(carry[:], 0.0), outs=[carry])
    for dt in range(NDT):
        p.emit("vector", lambda e, dt=dt: e.memset(
            x_fm[:, dt * (T + 3):dt * (T + 3) + 3], 0.0), outs=[x_fm])
    for f in b1_pieces(0):
        f()
    NU = 2 * NDT
    scan_prep(0, 0, 0, 0, rev=False, gu=0)
    scan_prep(0, 0, 0, 1, rev=False, gu=1)
    for c in range(NCH):
        g = c % 2
        nxt = b1_pieces(c + 1) if c + 1 < NCH else [None] * NDT
        for u in range(NU):
            gu = c * NU + u
            if u + 2 < NU:
                scan_prep(c, 0, g, u + 2, rev=False, gu=gu + 2)
            elif c + 1 < NCH:
                scan_prep(c + 1, 0, (c + 1) % 2, u + 2 - NU, rev=False, gu=gu + 2)
            scan_heavy(c, 0, g, u, rev=False, gu=gu)
            if u % 2 == 0 and nxt[u // 2] is not None:
                nxt[u // 2]()
            if u % 2 == 1:
                stage_y_fwd(c, u // 2)

    p.emit("vector", lambda e: e.memset(carry[:], 0.0), outs=[carry])
    for dt in range(NDT):
        p.emit("vector", lambda e, dt=dt: e.memset(
            x_fm[:, dt * (T + 3) + T:dt * (T + 3) + T + 3], 0.0), outs=[x_fm])
    for f in b2_pieces(NCH - 1):
        if f is not None:
            f()
    scan_prep(NCH - 1, 1, (NCH - 1) % 2, 0, rev=True, gu=0)
    scan_prep(NCH - 1, 1, (NCH - 1) % 2, 1, rev=True, gu=1)
    for c in range(NCH - 1, -1, -1):
        g = c % 2
        nxt = b2_pieces(c - 1) if c > 0 else [None] * NDT
        if c == NCH - 1:
            for dtp in range(3):
                yz_prefetch(c, dtp)
        stage_partial_prefetch(c)
        for u in range(NU):
            gu = (NCH - 1 - c) * NU + u
            if u % 2 == 1:
                stage_sig(c, u // 2)
            if u + 2 < NU:
                scan_prep(c, 1, g, u + 2, rev=True, gu=gu + 2)
            elif c > 0:
                scan_prep(c - 1, 1, (c - 1) % 2, u + 2 - NU, rev=True, gu=gu + 2)
            scan_heavy(c, 1, g, u, rev=True, gu=gu)
            if u % 2 == 0:
                if nxt[u // 2] is not None:
                    nxt[u // 2]()
                if u // 2 + 3 < NDT:
                    yz_prefetch(c, u // 2 + 3)
                elif c > 0:
                    yz_prefetch(c - 1, u // 2 + 3 - NDT)
            if u % 2 == 1:
                stage_y_bwd(c, u // 2)
        stage_outproj_final(c)

    with nc.Block() as block:
        @block.sync
        def _(e):
            for f in p.prog["sync"]:
                f(e)

        @block.tensor
        def _(e):
            for f in p.prog["tensor"]:
                f(e)

        @block.vector
        def _(e):
            for f in p.prog["vector"]:
                f(e)

        @block.scalar
        def _(e):
            for f in p.prog["scalar"]:
                f(e)

        @block.gpsimd
        def _(e):
            for f in p.prog["gpsimd"]:
                f(e)

    st.close()
    return nc


def host_inputs(core_inp, b):
    f = np.ascontiguousarray
    w_in = core_inp["L0_attn_in_proj_w"]
    w_out = core_inp["L0_attn_out_proj_w"]
    wq = core_inp["L0_ffn_weight_q"]
    return {
        "hx": f(core_inp["L0_attn_hidden_states"][b]),
        "w_in_t": f(w_in.reshape(32, 128, 8, 128).transpose(0, 2, 3, 1)),
        "w_out_t": f(w_out.reshape(8, 128, 16, 128).transpose(0, 2, 3, 1)),
        "wq_t": f(wq.reshape(8, 128, 8, 128).transpose(2, 0, 3, 1)),
        "xproj_t": f(np.stack([
            core_inp["L0_attn_x_proj_w"].reshape(96, 16, 128).transpose(1, 2, 0),
            core_inp["L0_attn_x_proj_b_w"].reshape(96, 16, 128).transpose(1, 2, 0)])),
        "dtw_t": f(np.stack([
            core_inp["L0_attn_dt_proj_w"].reshape(16, 128, 64).transpose(0, 2, 1),
            core_inp["L0_attn_dt_proj_b_w"].reshape(16, 128, 64).transpose(0, 2, 1)])),
        "wconv": f(np.stack([
            core_inp["L0_attn_conv1d_w"].reshape(16, 128, 4),
            core_inp["L0_attn_conv1d_b_w"].reshape(16, 128, 4)])),
        "cbias": f(np.stack([
            core_inp["L0_attn_conv1d_bias"].reshape(16, 128),
            core_inp["L0_attn_conv1d_b_bias"].reshape(16, 128)])),
        "dtbias": f(np.stack([
            core_inp["L0_attn_dt_proj_bias"].reshape(16, 128),
            core_inp["L0_attn_dt_proj_b_bias"].reshape(16, 128)])),
        "alog": f(np.stack([
            core_inp["L0_attn_A_log"].reshape(16, 128, 16),
            core_inp["L0_attn_A_b_log"].reshape(16, 128, 16)])),
        "dvec": f(np.stack([
            core_inp["L0_attn_D"].reshape(16, 128),
            core_inp["L0_attn_D_b"].reshape(16, 128)])),
        "wsc": f(core_inp["L0_ffn_weight_scale"].reshape(8, 128)),
        "fbias": f(core_inp["L0_ffn_bias"].reshape(8, 128)),
    }


def kernel(**inputs):
    nc = build_program()
    in_maps = [host_inputs(inputs, b) for b in range(8)]
    res = run_bass_kernel_spmd(nc, in_maps, core_ids=list(range(8)))
    return np.stack([np.asarray(r["out"]) for r in res.results], axis=0)
